# version 36
# baseline (speedup 1.0000x reference)
import contextlib
import numpy as np
import concourse.bass as bass
import concourse.mybir as mybir
from concourse.alu_op_type import AluOpType as Op

F32 = mybir.dt.float32
BF16 = mybir.dt.bfloat16
I32 = mybir.dt.int32
AF = mybir.ActivationFunctionType
AX = mybir.AxisListType

C = 1024
H = 16
HD = 64
L = 128
NT = 8
EPS_LN = 1e-5
EPS_GN = 0.00064
NEG_EXP_HALF = -float(np.exp(-0.5))
POS_EXP_HALF = float(np.exp(-0.5))
RND_MAGIC = 12582912.0
NLEV = 7


def ts(i, n):
    return slice(i * n, (i + 1) * n)


def qb(h):
    return (h // 2) * 8 * L + (h % 2) * 4 * L


def xsl(h):
    lo = (h % 2) * 512 + (h // 2) * HD
    return slice(lo, lo + HD)


def wsl(wt, ct, fs, n):
    return wt[:, ct * C + fs: ct * C + fs + n]


def build_nc(SEG, debug=False):
    import os as _os
    NCH = SEG // L
    KCH = int(_os.environ.get('KCH', str(NCH)))
    nc = bass.Bass("TRN2", target_bir_lowering=False, debug=debug)

    D = {}
    def dram(name, shape, dtype=F32, kind=None):
        if kind:
            D[name] = nc.dram_tensor(name, shape, dtype, kind=kind).ap()
        else:
            D[name] = nc.dram_tensor(name, shape, dtype).ap()
        return D[name]

    d_x = dram("x_seg", [SEG, C], kind="ExternalInput")
    d_par = nc.dram_tensor("par", [1, 1], I32, kind="ExternalInput").ap()
    dram("cc_in", [HD, H * HD])
    dram("cc_out", [2 * HD, H * HD])
    for nm, shp in [("xctx_col", [128, NT]),
                    ("W_r", [C, C]), ("W_k", [C, C]), ("W_v", [C, C]), ("W_o", [C, C]),
                    ("w1", [C, HD]), ("w2aug", [HD + 1, C]), ("a1", [C, HD]),
                    ("a2aug", [HD + 1, C]), ("g1", [C, 128]), ("g2", [128, C]),
                    ("mixv", [128, 6 * NT]), ("kkv", [128, NT]), ("kav", [128, NT]),
                    ("rkbd", [128, 2 * NT]), ("lnw_rep", [128, C]), ("lnb_rep", [128, C]),
                    ("bo2", [128, 2]), ("selbd", [2, 128]), ("ident", [128, 128]),
                    ("maskqg2", [128, 5 * L]),
                    ("ffn_kernel", [C, C]), ("ffn_a", [C, 16]), ("ffn_baug", [17, C]),
                    ("s0_in", [HD, H * HD])]:
        dram(nm, shp, kind="ExternalInput")
    d_out = dram("out_seg", [SEG, C], kind="ExternalOutput")
    d_send = dram("s_end", [HD, H * HD], kind="ExternalOutput")
    d_x2 = dram("x2_spill", [SEG, C])

    x_rows = d_x.rearrange("(n p) c -> n p c", p=L)
    out_rows = d_out.rearrange("(n p) c -> n p c", p=L)
    x2_rows = d_x2.rearrange("(n p) c -> n p c", p=L)

    es = contextlib.ExitStack()
    def sb(name, shape, dtype, **kw):
        return es.enter_context(nc.sbuf_tensor(name, shape, dtype, **kw))

    wr = sb("wr", [128, NT * C], BF16); wk = sb("wk", [128, NT * C], BF16)
    wv = sb("wv", [128, NT * C], BF16); wo = sb("wo", [128, NT * C], BF16)
    w1 = sb("w1s", [128, NT * HD], BF16)
    a1 = sb("a1s", [128, NT * HD], BF16)
    g1 = sb("g1s", [128, NT * 128], BF16)
    ffna = sb("ffna_s", [128, NT * 16], BF16)
    w2a = sb("w2a", [HD + 1, C], BF16); a2a = sb("a2a", [HD + 1, C], BF16)
    g2 = sb("g2s", [128, C], BF16)
    ffnb = sb("ffnb_s", [17, C], BF16)
    mixv = sb("mixv_s", [128, 6 * NT], F32)
    kkv = sb("kkv_s", [128, NT], F32); kav = sb("kav_s", [128, NT], F32)
    kav1m = sb("kav1m", [128, NT], F32)
    rkbd = sb("rkbd_s", [128, 2 * NT], BF16)
    lnw = sb("lnw_s", [128, C], BF16); lnb = sb("lnb_s", [128, C], BF16)
    bo2 = sb("bo2_s", [128, 2], BF16); selbd = sb("selbd_s", [2, 128], F32)
    maskqg2 = sb("maskqg2_s", [128, 5 * L], F32)
    xctxc = sb("xctxc", [128, NT], F32)
    xctx_hat = sb("xctx_hat", [128, NT], BF16)
    ss = sb("ss_f", [HD, H * HD], F32)
    ssb = sb("ssb", [128, H * HD], BF16)
    x_tm = [sb(f"x_tm{i}", [L, C], F32) for i in range(3)]
    xh_tm = sb("xh_tm", [L, C], BF16)
    xh_ch = [sb(f"xh_ch{p}", [128, NT * L], BF16, is_transpose_dest=True) for p in range(2)]
    t1c = sb("t1c", [128, NT * L], BF16)
    mix = {m: sb(f"mix_{m}", [128, NT * L], BF16) for m in "rkv"}
    mixS = sb("mixS", [128, NT * L], BF16)
    statsP = sb("statsP", [L, 16], F32)
    statsQ = sb("statsQ", [L, 16], F32)
    rbf = sb("rbf", [128, NT * L], BF16, is_transpose_dest=True)
    kbf = sb("kbf", [128, NT * L], BF16)
    ktil = sb("ktil", [128, NT * L], BF16)
    k2bf = sb("k2bf", [128, NT * L], BF16)
    abf = sb("abf", [128, NT * L], BF16)
    sigw = sb("sigw", [128, NT * L], BF16)
    lam = sb("lam", [128, NT * L], F32)
    rdch = sb("rdch", [128, NT * L], F32)
    rlam = sb("rlam", [128, NT * L], BF16)
    rnorm = sb("rnorm", [2, NT * L], F32)
    tanht = sb("tanht", [HD + 1, L], BF16)
    a1t = sb("a1t", [HD + 1, L], BF16)
    sgt = sb("sgt", [128, L], BF16)
    AR1 = sb("AR1", [128, NT * 2 * L], BF16)
    Bh1 = sb("Bh1", [128, NT * L], BF16)
    Kh1 = sb("Kh1", [128, NT * L], BF16)
    Btm1 = sb("Btm1", [L, C], BF16, is_transpose_dest=True)
    Ktm1 = sb("Ktm1", [L, C], BF16, is_transpose_dest=True)
    vtm = [sb(f"vtm{i}", [L, C], BF16) for i in range(2)]
    gtm = [sb(f"gtm{i}", [L, C], BF16) for i in range(2)]
    lamL = [sb(f"lamL{i}", [128, NT], F32) for i in range(2)]
    coef = [sb(f"coef{i}", [L, H], F32) for i in range(2)]
    QNG = sb("QNG", [128, H * 4 * L], BF16)
    Ppr = sb("Ppr", [128, H * L], BF16)
    Qpr = sb("Qpr", [128, H * L], BF16)
    Xs = sb("Xs", [L, H * HD], BF16)
    y_tm = sb("y_tm", [L, C], F32)
    ysq = sb("ysq", [L, C], F32)
    gnst = sb("gnst", [L, 4 * H], F32)
    yg_tm = sb("yg_tm", [L, C], BF16)
    yg_ch = sb("yg_ch", [128, NT * L], BF16, is_transpose_dest=True)
    att_tm = sb("att_tm", [L, C], F32)
    f_lor = sb("f_lor", [17, L], BF16)
    qm = sb("qmax", [128, 12], F32)
    eyebf = sb("eyebf", [128, 128], BF16)
    constt = sb("constt", [128, 6], F32)
    pt_sb = sb("pt_sb", [1, 1], I32)
    pt2_sb = sb("pt2_sb", [1, 1], I32)
    f_xh = xh_tm
    f_outb = [att_tm, lam]
    f_ok = ["att", "lam"]
    fxs = [x_tm[0], x_tm[1], x_tm[2], rdch]
    fxk = ["x0", "x1", "x2", "rdch"]
    f_xhch = [yg_ch, rbf, kbf, ktil]
    f_xk = ["ygch", "rbf", "kbf", "ktil"]
    ffnw = wr
    o_tmb = [y_tm, ysq]
    o_tk = ["y", "ysq"]
    nc.const_aps.register(F32, EPS_LN, constt[:, 0:1])
    nc.const_aps.register(F32, EPS_GN, constt[:, 1:2])
    nc.const_aps.register(F32, NEG_EXP_HALF, constt[:, 2:3])
    nc.const_aps.register(F32, POS_EXP_HALF, constt[:, 4:5])

    pbank = es.enter_context(nc.psum_tensor("pbank", [128, 8 * 512], F32))

    def bank(i, lo=0, n=512):
        return pbank[:, i * 512 + lo: i * 512 + lo + n]

    def bankp(i, p0, np_, lo=0, n=512):
        return pbank[p0:p0 + np_, i * 512 + lo: i * 512 + lo + n]

    def bankspan(i, n2):
        return pbank[:, i * 512: i * 512 + n2]

    NDQ = 12
    sem_names = (["sq", "gq", "te", "ve", "se", "ge", "cc", "s0q", "s0ready", "czq", "tq",
                  "ccin"]
                 + [f"dq{i}" for i in range(NDQ)])
    SH = {n: es.enter_context(nc.semaphore(n)) for n in sem_names}
    CN = {n: 0 for n in sem_names}

    _COMPUTE_OPS = {"tensor_tensor", "tensor_scalar", "scalar_tensor_tensor", "tensor_copy",
                    "tensor_reduce", "reduce_sum", "reduce_max", "reciprocal", "memset",
                    "bn_stats", "bn_aggr", "tensor_tensor_scan", "activation", "copy",
                    "square", "sqrt", "tensor_add", "tensor_mul", "tensor_sub",
                    "tensor_scalar_mul", "tensor_scalar_add", "select", "memzero",
                    "add", "mul"}

    sy = nc.sync
    gp_raw = nc.gpsimd
    te = nc.tensor
    ve = nc.vector
    se = nc.scalar
    gp = nc.gpsimd

    E = {"te": (te, "te", 1), "ve": (ve, "ve", 1), "se": (se, "se", 1),
         "ge": (gp, "ge", 1), "sq": (sy, "sq", 16), "gq": (gp_raw, "gq", 16)}

    wl = {}
    rl = {}
    waited = {k: {} for k in E}

    def _wait(ek, sem, cnt):
        eng = E[ek][0]
        own = E[ek][1]
        if sem == own and ek == "te":
            return
        if sem == "gq":
            cnt = CN[sem]
        if waited[ek].get(sem, -1) >= cnt:
            return
        eng.wait_ge(SH[sem], cnt)
        waited[ek][sem] = cnt

    def O(ek, fn, r=(), w=(), group=False):
        eng, sem, amt = E[ek]
        for b in w:
            if b in wl:
                _wait(ek, *wl[b])
            for s2, c2 in rl.get(b, {}).items():
                _wait(ek, s2, c2)
        for b in r:
            if b in wl:
                _wait(ek, *wl[b])
        instr = fn()
        CN[sem] += amt
        instr.then_inc(SH[sem], amt)
        for b in w:
            wl[b] = (sem, CN[sem])
            rl[b] = {}
        for b in r:
            rl.setdefault(b, {})[sem] = CN[sem]
        return instr

    dq_state = {"i": 0}

    def DQO(fn, r=(), w=(), tp=False):
        if tp:
            s = "tq"
        else:
            s = f"dq{dq_state['i'] % NDQ}"
            dq_state["i"] += 1
        if CN[s] > 0:
            _wait_raw(sy, "syeng", s, CN[s])
        for b in w:
            if b in wl:
                _wait_raw(sy, "syeng", *wl[b])
            for s2, c2 in rl.get(b, {}).items():
                _wait_raw(sy, "syeng", s2, c2)
        for b in r:
            if b in wl:
                _wait_raw(sy, "syeng", *wl[b])
        instr = fn()
        CN[s] += 16
        instr.then_inc(SH[s], 16)
        for b in w:
            wl[b] = (s, CN[s])
            rl[b] = {}
        for b in r:
            rl.setdefault(b, {})[s] = CN[s]
        return instr

    def _wait_raw(eng, ek, sem, cnt):
        if sem == "gq":
            cnt = CN[sem]
        if waited.setdefault(ek, {}).get(sem, -1) >= cnt:
            return
        eng.wait_ge(SH[sem], cnt)
        waited[ek][sem] = cnt

    class DirectEM:
        direct = True

        def O(self, *a, **k):
            k.pop("presplit", False)
            return O(*a, **k)

        def DQO(self, *a, **k):
            return DQO(*a, **k)

        def cut(self):
            pass

        def mark(self, name):
            pass

        def hold(self, name):
            pass

    class Rec:
        direct = False

        def __init__(self):
            self.atoms = []
            self.cur = []
            self.pending = set()
            self.marks = {}
            self.holds = {}

        def mark(self, name):
            self.cut()
            self.marks[name] = len(self.atoms)

        def hold(self, name):
            self.cut()
            self.holds[len(self.atoms)] = name

        def _append(self, it, k):
            if not self.pending:
                self.cut()
            self.cur.append(it)
            for b in k.get("r", ()):
                self.pending.discard(b)
            for b in k.get("w", ()):
                if b.startswith("bk"):
                    self.pending.add(b)

        def O(self, *a, **k):
            if k.pop("presplit", False):
                self.pending.clear()
                self.cut()
            self._append(("O", a, k), k)

        def DQO(self, *a, **k):
            self._append(("D", a, k), k)

        def cut(self):
            if self.cur:
                self.atoms.append(self.cur)
                self.cur = []

    DIR = DirectEM()

    def record(fn, c):
        r = Rec()
        fn(c, r)
        r.cut()
        return r

    def replay(item):
        t, a, k = item
        if t == "O":
            O(*a, **k)
        else:
            DQO(*a, **k)

    def merge(streams):
        idx = [0] * len(streams)
        marks = streams[0].marks
        live = True
        while live:
            live = False
            for i, s in enumerate(streams):
                if idx[i] < len(s.atoms):
                    nm = s.holds.get(idx[i])
                    if nm is not None and i != 0 and idx[0] < marks.get(nm, 0):
                        live = True
                        continue
                    for item in s.atoms[idx[i]]:
                        replay(item)
                    idx[i] += 1
                    live = True

    lp = nc.allow_low_precision(reason="bf16 matmul operands; fp32 accumulation kept")
    lp.__enter__()

    def ln_into(EM, dst_bf, src_f32, src_key, dst_key, st, stk, scr=None, sck=None):
        if scr is None:
            EM.O("ve", lambda: ve.bn_stats(st[:, 0:6], src_f32[:, 0:512]),
                 r=[src_key], w=[stk])
            EM.O("ve", lambda: ve.bn_stats(st[:, 6:12], src_f32[:, 512:1024]),
                 r=[src_key, stk], w=[stk])
            EM.O("ve", lambda: ve.bn_aggr(st[:, 12:14], st[:, 0:12]),
                 r=[stk], w=[stk])
        else:
            EM.O("se", lambda: se.activation(scr[:, :], src_f32[:, :], AF.Identity,
                                             accum_out=st[:, 12:13]),
                 r=[src_key], w=[stk, sck])
            EM.O("se", lambda: se.activation(scr[:, :], src_f32[:, :], AF.Square,
                                             accum_out=st[:, 13:14]),
                 r=[src_key, stk], w=[stk, sck])
            EM.O("ve", lambda: ve.tensor_scalar(st[:, 12:14], st[:, 12:14], 1.0 / C, None,
                                                Op.mult), r=[stk], w=[stk])
            EM.O("ve", lambda: ve.tensor_tensor(st[:, 15:16], st[:, 12:13], st[:, 12:13],
                                                Op.mult), r=[stk], w=[stk])
            EM.O("ve", lambda: ve.tensor_tensor(st[:, 13:14], st[:, 13:14], st[:, 15:16],
                                                Op.subtract), r=[stk], w=[stk])
        EM.O("se", lambda: se.activation(st[:, 14:15], st[:, 13:14], AF.Sqrt,
                                         bias=EPS_LN), r=[stk, "constt"], w=[stk])
        EM.O("ve", lambda: ve.reciprocal(st[:, 14:15], st[:, 14:15]), r=[stk], w=[stk])
        EM.O("ve", lambda: ve.scalar_tensor_tensor(st[:, 15:16], st[:, 12:13], -1.0,
                                                   st[:, 14:15], Op.mult, Op.mult),
             r=[stk], w=[stk])
        EM.O("se", lambda: se.activation(dst_bf[:, :], src_f32[:, :], AF.Identity,
                                         bias=st[:, 15:16], scale=st[:, 14:15]),
             r=[src_key, stk], w=[dst_key])

    PT_SEM = [None]

    def pe_transpose(EM, src, src_key, dst, dst_key, extra_r=(), b0=2):
        for bq in range(2):
            def grp_tx(bq=bq, src=src):
                for f in range(4):
                    ct = bq * 4 + f
                    ii = te.matmul(bank(b0 + bq, f * L, L), src[:, ts(ct, 128)],
                                   eyebf[:, :], start=True, stop=True)
                return ii
            EM.O("te", grp_tx, r=[src_key, "eye"] + list(extra_r), w=[f"bk{b0 + bq}"])
            EM.O("se", lambda bq=bq, dst=dst: se.copy(dst[:, ts(bq, 512)], bank(b0 + bq)),
                 r=[f"bk{b0 + bq}"], w=[dst_key])

    def emit_prep(c, EM):
        pr2, pr3 = c % 2, c % 3
        xk = f"xh_ch{pr2}"
        xkp = f"xh_ch{1 - pr2}"
        ark = "AR"
        AR = AR1
        xkey = f"x{pr3}"
        vk = f"vtm{pr2}"
        gk = f"gtm{pr2}"
        ck = f"coef{pr2}"
        lk = f"lamL{pr2}"
        EM.DQO(lambda c=c, pr3=pr3: sy.dma_start(out=x_tm[pr3][:, :], in_=x_rows[c]),
               w=[xkey])
        ln_into(EM, xh_tm, x_tm[pr3], xkey, "xh_tm", statsP, "statsP",
                scr=t1c, sck="t1c")
        pe_transpose(EM, xh_tm, "xh_tm", xh_ch[pr2], xk, b0=6)
        if c == 0:
            assert EM.direct
            with sy.register("ps_r") as ps_r:
                sy.wait_ge(SH[PT_SEM[0][0]], PT_SEM[0][1])
                ss_sem = wl["ss"]
                sy.wait_ge(SH[ss_sem[0]], ss_sem[1])
                sy.reg_load(ps_r, pt_sb[0:1, 0:1])
                with sy.If_eq(ps_r, 1):
                    sy.wait_ge(SH["cc"], 1)
                    sy.dma_start(out=ss[:, :], in_=D["cc_out"][0:HD, :]).then_inc(SH["s0q"], 16)
                    sy.wait_ge(SH["s0q"], 16)
                sy.dma_start(out=pt2_sb[:, :], in_=d_par).then_inc(SH["s0ready"], 16)
            CN["s0ready"] += 16
            wl["ss"] = ("s0ready", 16)
        t1v = t1c.ap().rearrange("p (i l) -> p i l", l=L)
        xhv = xh_ch[pr2].ap().rearrange("p (i l) -> p i l", l=L)
        EM.O("ve", lambda t1v=t1v, xhv=xhv: ve.tensor_tensor(
            t1v[:, :, 1:L], xhv[:, :, 0:L - 1], xhv[:, :, 1:L], Op.subtract),
            r=[xk], w=["t1c"])
        if c == 0:
            prev_col = xctx_hat.ap().rearrange("p (i one) -> p i one", one=1)
            pkey = "xctxh"
        else:
            prev_col = xh_ch[1 - pr2].ap().rearrange("p (i l) -> p i l", l=L)[:, :, L - 1:L]
            pkey = xkp
        EM.O("ve", lambda t1v=t1v, xhv=xhv, prev_col=prev_col: ve.tensor_tensor(
            t1v[:, :, 0:1], prev_col, xhv[:, :, 0:1], Op.subtract),
            r=[xk, pkey, "t1c"], w=["t1c"])

        t3 = t1c.ap().rearrange("p (i l) -> p i l", l=L)
        x3 = xh_ch[pr2].ap().rearrange("p (i l) -> p i l", l=L)

        def emit_mix(dst, dk, mi, extra=()):
            d3 = dst.ap().rearrange("p (i l) -> p i l", l=L)
            m3 = mixv.ap()[:, mi * NT:(mi + 1) * NT].rearrange(
                "p (i one) -> p i one", one=1).broadcast_to((128, NT, L))
            EM.O("ve", lambda d3=d3, m3=m3: ve.tensor_tensor(d3, t3, m3, Op.mult),
                 r=["t1c", "mixv"] + list(extra), w=[dk])
            EM.O("ve", lambda d3=d3: ve.tensor_tensor(d3, d3, x3, Op.add),
                 r=[xk, dk], w=[dk])

        emit_mix(mixS, "mxS", 1)
        def grp_w1():
            for ct in range(NT):
                ii = te.matmul(bankp(6, 0, HD, 0, L), w1[:, ts(ct, HD)],
                               mixS[:, ts(ct, L)], start=(ct == 0), stop=(ct == NT - 1))
            return ii
        EM.O("te", grp_w1, r=["mxS", "w1"], w=["bk6"])
        EM.O("se", lambda: se.activation(tanht[0:HD, :], bankp(6, 0, HD, 0, L), AF.Tanh),
             r=["bk6"], w=["tanht"])
        emit_mix(mixS, "mxS", 4)
        def grp_a1():
            for ct in range(NT):
                ii = te.matmul(bankp(6, 0, HD, L, L), a1[:, ts(ct, HD)],
                               mixS[:, ts(ct, L)], start=(ct == 0), stop=(ct == NT - 1))
            return ii
        EM.O("te", grp_a1, r=["mxS", "a1"], w=["bk6"])
        EM.O("se", lambda: se.copy(a1t[0:HD, :], bankp(6, 0, HD, L, L)),
             r=["bk6"], w=["a1t"])
        emit_mix(mixS, "mxS", 5)
        def grp_g1():
            for ct in range(NT):
                ii = te.matmul(bank(6, 2 * L, L), g1[:, ts(ct, 128)],
                               mixS[:, ts(ct, L)], start=(ct == 0), stop=(ct == NT - 1))
            return ii
        EM.O("te", grp_g1, r=["mxS", "g1"], w=["bk6"])
        EM.O("se", lambda: se.activation(sgt[:, :], bank(6, 2 * L, L), AF.Sigmoid),
             r=["bk6"], w=["sgt"])
        for g4 in range(2):
            def grp_z(g4=g4):
                for f4 in range(4):
                    ft = g4 * 4 + f4
                    te.matmul(bank(7, f4 * L, L), w2a[:, ts(ft, 128)], tanht[:, :],
                              start=True, stop=True)
                    ii = te.matmul(bank(6, f4 * L, L), a2a[:, ts(ft, 128)], a1t[:, :],
                                   start=True, stop=True)
                return ii
            EM.O("te", grp_z, r=["tanht", "tanht_one", "w2a", "a1t", "a1t_one", "a2a"],
                 w=["bk7", "bk6"])
            EM.O("se", lambda g4=g4: se.activation(sigw[:, ts(g4, 512)], bank(7),
                                                   AF.Sigmoid), r=["bk7"], w=["sigw"])
            EM.O("se", lambda g4=g4: se.activation(abf[:, ts(g4, 512)], bank(6),
                                                   AF.Sigmoid), r=["bk6"], w=["abf"])
        for half in range(2):
            EM.O("te", lambda half=half: te.matmul(bank(7), sgt[:, :], g2[:, ts(half, 512)],
                                                   start=True, stop=True),
                 r=["sgt", "g2"], w=["bk7"])
            EM.O("se", lambda half=half, pr2=pr2: se.copy(gtm[pr2][:, ts(half, 512)],
                                                          bank(7)),
                 r=["bk7"], w=[gk])
        EM.O("se", lambda: se.activation(ysq[:, :], sigw[:, :], AF.Exp, scale=NEG_EXP_HALF),
             r=["sigw", "constt"], w=["ysq"])
        dv = ysq.ap().rearrange("p (i l) -> p i l", l=L)
        for i_t in range(NT):
            EM.O("ve", lambda i_t=i_t, dv=dv: ve.tensor_tensor_scan(lam[:, ts(i_t, L)],
                                                                    dv[:, i_t], dv[:, i_t], 1.0,
                                                                    Op.mult, Op.bypass),
                 r=["ysq"], w=["lam"])
        EM.O("se", lambda: se.activation(rdch[:, :], sigw[:, :], AF.Exp, scale=POS_EXP_HALF),
             r=["sigw", "constt"], w=["rdch"])
        dv2 = rdch.ap().rearrange("p (i l) -> p i l", l=L)
        for i_t in range(NT):
            EM.O("ve", lambda i_t=i_t, dv2=dv2: ve.tensor_tensor_scan(rlam[:, ts(i_t, L)],
                                                                      dv2[:, i_t], dv2[:, i_t],
                                                                      1.0, Op.mult, Op.bypass),
                 r=["rdch"], w=["rlam"])
        lam_last = lam.ap().rearrange("p (i l) -> p i l", l=L)[:, :, L - 1:L]
        EM.O("ve", lambda lam_last=lam_last, pr2=pr2: ve.tensor_copy(
            lamL[pr2].ap().rearrange("p (i one) -> p i one", one=1), lam_last),
            r=["lam"], w=[lk])
        emit_mix(mix["k"], "mxk", 2)
        for bk2 in range(2):
            b_i = 6 + bk2
            def grpk(b_i=b_i, bk2=bk2):
                for f4 in range(4):
                    ft = bk2 * 4 + f4
                    for ct in range(NT):
                        ii = te.matmul(bank(b_i, f4 * L, L), wsl(wk, ct, ft * 128, 128),
                                       mix["k"][:, ts(ct, L)],
                                       start=(ct == 0), stop=(ct == NT - 1))
                return ii
            EM.O("te", grpk, r=["mxk", "wk"], w=[f"bk{b_i}"])
            EM.O("se", lambda bk2=bk2, b_i=b_i: se.copy(kbf[:, ts(bk2, 4 * L)], bank(b_i)),
                 r=[f"bk{b_i}"], w=["kbf"])
        emit_mix(mix["r"], "mxr", 0)
        for bk2 in range(2):
            b_i = 6 + bk2
            def grpr(b_i=b_i, bk2=bk2):
                for f4 in range(4):
                    ft = bk2 * 4 + f4
                    for ct in range(NT):
                        ii = te.matmul(bank(b_i, f4 * L, L), wsl(wr, ct, ft * 128, 128),
                                       mix["r"][:, ts(ct, L)],
                                       start=(ct == 0), stop=(ct == NT - 1))
                return ii
            EM.O("te", grpr, r=["mxr", "wr"], w=[f"bk{b_i}"])
            EM.O("se", lambda bk2=bk2, b_i=b_i: se.copy(rbf[:, ts(bk2, 4 * L)], bank(b_i)),
                 r=[f"bk{b_i}"], w=["rbf"])
        emit_mix(mix["v"], "mxv", 3)
        for half in range(2):
            def grpv(half=half):
                for ct in range(NT):
                    ii = te.matmul(bank(6 + half), mix["v"][:, ts(ct, L)],
                                   wsl(wv, ct, half * 512, 512),
                                   start=(ct == 0), stop=(ct == NT - 1))
                return ii
            EM.O("te", grpv, r=["mxv", "wv"], w=[f"bk{6 + half}"])
            EM.O("se", lambda half=half, pr2=pr2: se.copy(vtm[pr2][:, ts(half, 512)],
                                                          bank(6 + half)),
                 r=[f"bk{6 + half}"], w=[vk])
        kt3 = ktil.ap().rearrange("p (i l) -> p i l", l=L)
        kb3 = kbf.ap().rearrange("p (i l) -> p i l", l=L)
        kkv3 = kkv.ap().rearrange("p (i one) -> p i one", one=1).broadcast_to((128, NT, L))
        EM.O("ve", lambda: ve.tensor_tensor(kt3, kb3, kkv3, Op.mult),
             r=["kbf", "kkv"], w=["ktil"])
        EM.O("ve", lambda: ve.tensor_tensor(sigw[:, :], ktil[:, :], ktil[:, :], Op.mult),
             r=["ktil"], w=["sigw"])
        for g2i in range(2):
            b_i = 6 + g2i
            EM.O("te", lambda g2i=g2i, b_i=b_i: te.matmul(
                bankp(b_i, 0, 2, 0, 4 * L), bo2[:, :], sigw[:, ts(g2i, 4 * L)],
                start=True, stop=True),
                r=["sigw", "bo2"], w=[f"bk{b_i}"])
        for b_i in (6, 7):
            EM.O("se", lambda b_i=b_i: se.activation(
                rnorm[:, (b_i - 6) * 4 * L:(b_i - 6) * 4 * L + 4 * L],
                bankp(b_i, 0, 2, 0, 4 * L), AF.Sqrt),
                r=[f"bk{b_i}"], w=["rnorm"])
        EM.O("ve", lambda: ve.tensor_scalar(rnorm[:, :], rnorm[:, :], 1e-12, None, Op.max),
             r=["rnorm"], w=["rnorm"])
        EM.O("ve", lambda: ve.reciprocal(rnorm[:, :], rnorm[:, :]),
             r=["rnorm"], w=["rnorm"])
        for i_t in range(NT):
            b_i = 6 + (i_t % 2)
            EM.O("te", lambda i_t=i_t, b_i=b_i: te.matmul(bank(b_i, 0, L), selbd[:, :],
                                                          rnorm[:, ts(i_t, L)],
                                                          start=True, stop=True),
                 r=["rnorm", "selbd"], w=[f"bk{b_i}"])
            EM.O("ve", lambda i_t=i_t, b_i=b_i: ve.tensor_tensor(ktil[:, ts(i_t, L)],
                                                                 bank(b_i, 0, L),
                                                                 ktil[:, ts(i_t, L)], Op.mult),
                 r=[f"bk{b_i}"], w=["ktil"])
        k23 = k2bf.ap().rearrange("p (i l) -> p i l", l=L)
        ab3 = abf.ap().rearrange("p (i l) -> p i l", l=L)
        kav3 = kav.ap().rearrange("p (i one) -> p i one", one=1).broadcast_to((128, NT, L))
        kav1m3 = kav1m.ap().rearrange("p (i one) -> p i one", one=1).broadcast_to((128, NT, L))
        EM.O("ve", lambda: ve.tensor_tensor(k23, ab3, kav3, Op.mult),
             r=["abf", "kav"], w=["k2bf"])
        EM.O("ve", lambda: ve.tensor_tensor(k23, k23, kav1m3, Op.add),
             r=["kav1m", "k2bf"], w=["k2bf"])
        EM.O("ve", lambda: ve.tensor_tensor(k2bf[:, :], k2bf[:, :], kbf[:, :], Op.mult),
             r=["k2bf", "kbf"], w=["k2bf"])
        EM.O("ve", lambda: ve.tensor_tensor(sigw[:, :], rbf[:, :], k2bf[:, :], Op.mult),
             r=["rbf", "k2bf"], w=["sigw"])
        def grp_coef():
            for i_t in range(NT):
                ii = te.matmul(bank(6, i_t * 2, 2), sigw[:, ts(i_t, L)],
                               rkbd[:, 2 * i_t:2 * i_t + 2], start=True, stop=True)
            return ii
        EM.O("te", grp_coef, r=["sigw", "rkbd"], w=["bk6"])
        EM.O("ve", lambda pr2=pr2: ve.tensor_copy(coef[pr2][:, :], bank(6, 0, 2 * NT)),
             r=["bk6"], w=[ck])
        EM.hold("yred")
        ar_v = AR.ap().rearrange("p (i l2) -> p i l2", l2=2 * L)
        lam_v = lam.ap().rearrange("p (i l) -> p i l", l=L)
        kk_v = ktil.ap().rearrange("p (i l) -> p i l", l=L)
        EM.O("ve", lambda ar_v=ar_v, lam_v=lam_v, kk_v=kk_v: ve.scalar_tensor_tensor(
            ar_v[:, :, 1:L], lam_v[:, :, 0:L - 1], -1.0, kk_v[:, :, 1:L], Op.mult, Op.mult),
            r=["lam", "ktil"], w=[ark])
        EM.O("ve", lambda ar_v=ar_v, kk_v=kk_v: ve.scalar_tensor_tensor(
            ar_v[:, :, 0:1], kk_v[:, :, 0:1], -1.0, kk_v[:, :, 0:1], Op.mult, Op.bypass),
            r=["ktil", ark], w=[ark])
        r_v = rbf.ap().rearrange("p (i l) -> p i l", l=L)
        EM.O("ve", lambda ar_v=ar_v, r_v=r_v, lam_v=lam_v: ve.tensor_tensor(
            ar_v[:, :, L:2 * L], r_v, lam_v, Op.mult),
            r=["rbf", "lam", ark], w=[ark])
        EM.hold("gram")
        EM.O("ve", lambda: ve.tensor_tensor(Bh1[:, :], ktil[:, :], abf[:, :], Op.mult),
             r=["ktil", "abf"], w=["Bh"])
        EM.O("ve", lambda: ve.tensor_tensor(Bh1[:, :], Bh1[:, :], rlam[:, :], Op.mult),
             r=["Bh", "rlam"], w=["Bh"])
        EM.O("ve", lambda: ve.tensor_tensor(Kh1[:, :], k2bf[:, :], rlam[:, :], Op.mult),
             r=["k2bf", "rlam"], w=["Kh"])
        EM.hold("bkt")
        pe_transpose(EM, Bh1, "Bh", Btm1, "Btm", b0=6)
        pe_transpose(EM, Kh1, "Kh", Ktm1, "Ktm", b0=6)

    def emit_scan(c, EM):
        pr2 = c % 2
        SPL = (c != KCH - 1) and bool(int(_os.environ.get('KSPL', '0')) )
        KSPLM = int(_os.environ.get('KSPL', '0'))
        ark = "AR"
        AR = AR1
        vk = f"vtm{pr2}"
        lk = f"lamL{pr2}"
        vt = vtm[pr2]
        if c == 0:
            EM.O("se", lambda: se.copy(ssb[0:HD, :], ss[:, :]), r=["ss"], w=["ssb"])
            EM.O("se", lambda: se.copy(ssb[HD:128, :], ss[:, :]), r=["ss", "ssb"], w=["ssb"])
        for hp in range(H // 2):
            i_t = hp
            Ab = 0 if hp % 2 == 0 else 4
            for ph in range(2):
                prt = slice(ph * HD, ph * HD + HD)
                Asl = AR[prt, i_t * 2 * L: i_t * 2 * L + L]
                ARsl = AR[prt, i_t * 2 * L: (i_t + 1) * 2 * L]
                Bsl = Bh1[prt, ts(i_t, L)]
                Ksl = Kh1[prt, ts(i_t, L)]
                EM.O("te", lambda Bsl=Bsl, ARsl=ARsl, ph=ph, Ab=Ab:
                     te.matmul(bank(Ab + ph, 0, 2 * L), Bsl, ARsl, start=True, stop=True),
                     r=["Bh", ark], w=[f"bk{Ab + ph}"])
                EM.O("te", lambda Ksl=Ksl, ARsl=ARsl, ph=ph, Ab=Ab:
                     te.matmul(bank(Ab + ph, 2 * L, 2 * L), Ksl, ARsl, start=True, stop=True),
                     r=["Kh", ark], w=[f"bk{Ab + ph}"])
                EM.O("te", lambda Asl=Asl, Bsl=Bsl, ph=ph:
                     te.matmul(bank(2 + ph, 0, L), Asl, Bsl, start=True, stop=True),
                     r=[ark, "Bh"], w=[f"bk{2 + ph}"])
            pq = pbank.ap()[:, Ab * 512:(Ab + 2) * 512].rearrange(
                "p (b x) -> p b x", x=512)
            EM.O("ve", lambda hp=hp, pq=pq, Ab=Ab: ve.tensor_tensor(
                QNG.ap()[:, hp * 8 * L:(hp + 1) * 8 * L].rearrange("p (b x) -> p b x", x=4 * L),
                pq, maskqg2.ap()[:, 0:4 * L].rearrange(
                    "p (one x) -> p one x", one=1).broadcast_to((128, 2, 4 * L)), Op.mult),
                r=[f"bk{Ab}", f"bk{Ab + 1}", "maskqg"], w=["QNG"],
                presplit=SPL and bool(KSPLM & 1))
            pq3 = pbank.ap()[:, 2 * 512:4 * 512].rearrange("p (b x) -> p b x", x=512)[:, :, 0:L]
            EM.O("ve", lambda hp=hp, pq3=pq3: ve.tensor_tensor(
                Ppr.ap()[:, hp * 2 * L:(hp + 1) * 2 * L].rearrange("p (b x) -> p b x", x=L),
                pq3, maskqg2.ap()[:, 4 * L:5 * L].rearrange(
                    "p (one l) -> p one l", one=1).broadcast_to((128, 2, L)), Op.mult),
                r=["bk2", "bk3", "maskqg"], w=["Ppr"])
        EM.mark("gram")
        for h in range(H):
            b_i = 4 + h % 2
            bknm = f"bk{b_i}"
            prt = slice((h % 2) * HD, (h % 2) * HD + HD)
            i_t = h // 2
            Asl = AR[prt, i_t * 2 * L: i_t * 2 * L + L]
            def grp_rt(h=h, b_i=b_i, Asl=Asl, prt=prt):
                te.matmul(bank(b_i, (h // 2) * HD, HD), Asl, ssb[prt, ts(h, HD)],
                          start=True, stop=False)
                return te.matmul(bank(b_i, (h // 2) * HD, HD),
                                 QNG[:, qb(h) + 2 * L: qb(h) + 3 * L],
                                 vt[:, ts(h, HD)], start=False, stop=True)
            EM.O("te", grp_rt, r=[ark, "ssb", "QNG", vk], w=[bknm], group=True)
        EM.O("se", lambda: se.copy(Xs[:, 0:1024], bankspan(4, 1024)),
             r=["bk4", "bk5"], w=["Xs"])
        QvA = lambda h: QNG[:, qb(h): qb(h) + L]
        PvB = lambda h: QNG[:, qb(h) + 2 * L: qb(h) + 3 * L]
        for j in range(NLEV):
            even = (j % 2 == 0)
            qk = "QNG" if even else "Qpr"
            pk = "Ppr" if even else "QNG"
            Qsrc = QvA if even else (lambda h: Qpr[:, ts(h, L)])
            Psrc = (lambda h: Ppr[:, ts(h, L)]) if even else PvB
            for par_h in range(2):
                b_i = 4 + par_h
                def grp_x(par_h=par_h, b_i=b_i, Qsrc=Qsrc):
                    for k8 in range(8):
                        h = 2 * k8 + par_h
                        ii = te.matmul(bank(b_i, k8 * HD, HD), Qsrc(h), Xs[:, xsl(h)],
                                       start=True, stop=True)
                    return ii
                EM.O("te", grp_x, r=[qk, "Xs"], w=[f"bk{b_i}"])
            EM.O("ve", lambda: ve.tensor_tensor(Xs[:, 0:1024], bankspan(4, 1024),
                                                Xs[:, 0:1024], Op.add),
                 r=["bk4", "bk5", "Xs"], w=["Xs"], presplit=SPL and bool(KSPLM & 2))
            if j < NLEV - 1:
                pok = "QNG" if even else "Ppr"
                qok = "Qpr" if even else "QNG"
                for gg in range(2):
                    for grpi in (2 * gg, 2 * gg + 1):
                        b_p = grpi % 2
                        b_q = 2 + grpi % 2
                        def grp_sq(grpi=grpi, b_p=b_p, b_q=b_q, Qsrc=Qsrc, Psrc=Psrc):
                            for k4 in range(4):
                                h = 4 * grpi + k4
                                te.matmul(bank(b_p, k4 * L, L), Qsrc(h), Psrc(h),
                                          start=True, stop=True)
                                ii = te.matmul(bank(b_q, k4 * L, L), Psrc(h), Qsrc(h),
                                               start=True, stop=True)
                            return ii
                        EM.O("te", grp_sq, r=[qk, pk], w=[f"bk{b_p}", f"bk{b_q}"])
                    if even:
                        pdst = QNG.ap()[:, gg * 32 * L:(gg + 1) * 32 * L].rearrange(
                            "p (b x) -> p b x", x=4 * L)[:, :, 2 * L:3 * L]
                        qdst = Qpr[:, gg * 1024: gg * 1024 + 1024]
                        psrcv = bankspan(0, 1024).rearrange("p (b x) -> p b x", x=L)
                        EM.O("ve", lambda pdst=pdst, psrcv=psrcv: ve.tensor_copy(pdst, psrcv),
                             r=["bk0", "bk1"], w=[pok], presplit=SPL and bool(KSPLM & 4))
                        EM.O("se", lambda qdst=qdst: se.copy(qdst, bankspan(2, 1024)),
                             r=["bk2", "bk3"], w=[qok])
                    else:
                        pdst = Ppr[:, gg * 1024: gg * 1024 + 1024]
                        qdst = QNG.ap()[:, gg * 32 * L:(gg + 1) * 32 * L].rearrange(
                            "p (b x) -> p b x", x=4 * L)[:, :, 0:L]
                        qsrcv = bankspan(2, 1024).rearrange("p (b x) -> p b x", x=L)
                        EM.O("ve", lambda pdst=pdst: ve.tensor_copy(pdst, bankspan(0, 1024)),
                             r=["bk0", "bk1"], w=[pok], presplit=SPL and bool(KSPLM & 4))
                        EM.O("se", lambda qdst=qdst, qsrcv=qsrcv: se.copy(qdst, qsrcv),
                             r=["bk2", "bk3"], w=[qok])
        for h in range(H):
            b_i = 4 + h % 2
            prt = slice((h % 2) * HD, (h % 2) * HD + HD)
            i_t = h // 2
            Rsl = AR[prt, i_t * 2 * L + L: (i_t + 1) * 2 * L]
            def grp_y(h=h, b_i=b_i, Rsl=Rsl, prt=prt):
                te.matmul(bank(b_i, (h // 2) * HD, HD), Rsl, ssb[prt, ts(h, HD)],
                          start=True, stop=False)
                te.matmul(bank(b_i, (h // 2) * HD, HD),
                          QNG[:, qb(h) + L: qb(h) + 2 * L], Xs[:, xsl(h)],
                          start=False, stop=False)
                return te.matmul(bank(b_i, (h // 2) * HD, HD),
                                 QNG[:, qb(h) + 3 * L: qb(h) + 4 * L],
                                 vt[:, ts(h, HD)], start=False, stop=True)
            EM.O("te", grp_y, r=[ark, "ssb", "QNG", "Xs", vk], w=[f"bk{b_i}"])
        yv4 = y_tm.ap().rearrange("t (k two d) -> t two k d", two=2, d=HD)
        ysrc4 = bankspan(4, 1024).rearrange("p (b k d) -> p b k d", b=2, d=HD)
        EM.O("ve", lambda yv4=yv4, ysrc4=ysrc4: ve.tensor_copy(yv4, ysrc4),
             r=["bk4", "bk5"], w=["y"])
        EM.mark("yred")
        for half8 in range(2):
            b_i = half8
            def grp_ss(half8=half8, b_i=b_i):
                for k8 in range(8):
                    h = half8 * 8 + k8
                    te.matmul(bankp(b_i, 0, HD, k8 * HD, HD), Btm1[:, ts(h, HD)],
                              Xs[:, xsl(h)], start=True, stop=False)
                    ii = te.matmul(bankp(b_i, 0, HD, k8 * HD, HD), Ktm1[:, ts(h, HD)],
                                   vt[:, ts(h, HD)], start=False, stop=True)
                return ii
            EM.O("te", grp_ss, r=["Btm", "Ktm", "Xs", vk], w=[f"bk{b_i}"])
        EM.O("ve", lambda: ve.tensor_tensor(ss[:, 0:1024], pbank[0:HD, 0:1024],
                                            ss[:, 0:1024], Op.add),
             r=["bk0", "bk1"], w=["ss"], presplit=SPL and bool(KSPLM & 8))
        for h in range(H):
            lcol = lamL[pr2][(h % 2) * HD:(h % 2) * HD + HD, h // 2: h // 2 + 1]
            EM.O("ve", lambda h=h, lcol=lcol: ve.tensor_scalar(ss[:, ts(h, HD)],
                                                               ss[:, ts(h, HD)], lcol, None,
                                                               Op.mult),
                 r=[lk], w=["ss"])
        EM.O("se", lambda: se.copy(ssb[0:HD, :], ss[:, :]), r=["ss"], w=["ssb"])
        EM.O("se", lambda: se.copy(ssb[HD:128, :], ss[:, :]), r=["ss", "ssb"], w=["ssb"])
        EM.mark("bkt")

    def emit_post(c, EM):
        pr2, pr3 = c % 2, c % 3
        xkey = f"x{pr3}"
        vk = f"vtm{pr2}"
        gk = f"gtm{pr2}"
        ck = f"coef{pr2}"
        EM.O("se", lambda: se.activation(ysq[:, :], y_tm[:, :], AF.Square),
             r=["y"], w=["ysq"])
        y3 = y_tm.ap().rearrange("t (h d) -> t h d", d=HD)
        ysq3 = ysq.ap().rearrange("t (h d) -> t h d", d=HD)
        g3a = gnst.ap()[:, 0:H].rearrange("t (h one) -> t h one", one=1)
        g3b = gnst.ap()[:, H:2 * H].rearrange("t (h one) -> t h one", one=1)
        EM.O("ve", lambda y3=y3, g3a=g3a: ve.tensor_reduce(g3a, y3, AX.X, Op.add),
             r=["y"], w=["gnst"])
        EM.O("ve", lambda ysq3=ysq3, g3b=g3b: ve.tensor_reduce(g3b, ysq3, AX.X, Op.add),
             r=["ysq"], w=["gnst"])
        cf_bc = coef[pr2].ap().rearrange("t (h one) -> t h one", one=1).broadcast_to((L, H, HD))
        v3 = vtm[pr2].ap().rearrange("t (h d) -> t h d", d=HD)
        EM.O("ve", lambda ysq3=ysq3, v3=v3, cf_bc=cf_bc: ve.tensor_tensor(
            ysq3, v3, cf_bc, Op.mult), r=[vk, ck, "ysq"], w=["ysq"])
        EM.O("ve", lambda: ve.tensor_scalar(gnst[:, 0:H], gnst[:, 0:H], 1.0 / HD, None,
                                            Op.mult), r=["gnst"], w=["gnst"])
        EM.O("ve", lambda: ve.tensor_scalar(gnst[:, H:2 * H], gnst[:, H:2 * H], 1.0 / HD, None,
                                            Op.mult), r=["gnst"], w=["gnst"])
        EM.O("ve", lambda: ve.tensor_tensor(gnst[:, 2 * H:3 * H], gnst[:, 0:H], gnst[:, 0:H],
                                            Op.mult), r=["gnst"], w=["gnst"])
        EM.O("ve", lambda: ve.tensor_tensor(gnst[:, H:2 * H], gnst[:, H:2 * H],
                                            gnst[:, 2 * H:3 * H], Op.subtract),
             r=["gnst"], w=["gnst"])
        EM.O("se", lambda: se.activation(gnst[:, H:2 * H], gnst[:, H:2 * H], AF.Sqrt,
                                         bias=EPS_GN), r=["gnst", "constt"], w=["gnst"])
        EM.O("ve", lambda: ve.reciprocal(gnst[:, H:2 * H], gnst[:, H:2 * H]),
             r=["gnst"], w=["gnst"])
        EM.O("ve", lambda: ve.scalar_tensor_tensor(gnst[:, 0:H], gnst[:, 0:H], -1.0,
                                                   gnst[:, H:2 * H], Op.mult, Op.mult),
             r=["gnst"], w=["gnst"])
        rs_bc = gnst.ap()[:, H:2 * H].rearrange("t (h one) -> t h one", one=1).broadcast_to((L, H, HD))
        ms_bc = gnst.ap()[:, 0:H].rearrange("t (h one) -> t h one", one=1).broadcast_to((L, H, HD))
        EM.O("ve", lambda y3=y3, rs_bc=rs_bc: ve.tensor_tensor(y3, y3, rs_bc, Op.mult),
             r=["y", "gnst"], w=["y"])
        EM.O("ve", lambda y3=y3, ms_bc=ms_bc: ve.tensor_tensor(y3, y3, ms_bc, Op.add),
             r=["y", "gnst"], w=["y"])
        EM.O("ve", lambda: ve.tensor_tensor(y_tm[:, :], y_tm[:, :], lnw[:, :], Op.mult),
             r=["y", "lnw"], w=["y"])
        EM.O("ve", lambda: ve.tensor_tensor(y_tm[:, :], y_tm[:, :], lnb[:, :], Op.add),
             r=["y", "lnb"], w=["y"])
        EM.O("ve", lambda: ve.tensor_tensor(y_tm[:, :], y_tm[:, :], ysq[:, :], Op.add),
             r=["y", "ysq"], w=["y"])
        EM.O("ve", lambda pr2=pr2: ve.tensor_tensor(yg_tm[:, :], y_tm[:, :], gtm[pr2][:, :],
                                                    Op.mult),
             r=["y", gk], w=["yg"])
        pe_transpose(EM, yg_tm, "yg", yg_ch, "ygch")
        for half in range(2):
            b_i = 2 + half
            def grp_wo(half=half, b_i=b_i):
                for ct in range(NT):
                    ii = te.matmul(bank(b_i), yg_ch[:, ts(ct, L)],
                                   wsl(wo, ct, half * 512, 512),
                                   start=(ct == 0), stop=(ct == NT - 1))
                return ii
            EM.O("te", grp_wo, r=["ygch", "wo"], w=[f"bk{b_i}"])
            EM.O("se", lambda half=half, b_i=b_i: se.copy(att_tm[:, ts(half, 512)], bank(b_i)),
                 r=[f"bk{b_i}"], w=["att"])
        ln_into(EM, att_tm, att_tm, "att", "att", statsQ, "statsQ",
                scr=yg_tm, sck="yg")
        EM.O("ve", lambda pr3=pr3: ve.tensor_tensor(att_tm[:, :], x_tm[pr3][:, :],
                                                    att_tm[:, :], Op.add),
             r=[xkey, "att"], w=["att"])
        EM.DQO(lambda c=c: sy.dma_start(out=x2_rows[c], in_=att_tm[:, :]),
               r=["att"], w=[f"x2r{c}"])

    def emit_quant(c, EM):
        islot = (KCH - 3) % 3
        identv = x_tm[islot][:, 0:128]
        EM.DQO(lambda identv=identv: sy.dma_start(out=identv, in_=D["ident"]),
               w=[f"x{islot}"])
        kblocks = D["ffn_kernel"].rearrange("(j p) c -> j p c", p=128)
        for j in range(NT):
            EM.DQO(lambda j=j: sy.dma_start(out=lam[:, 0:C], in_=kblocks[j]), w=["lam"])
            EM.O("ve", lambda j=j: ve.tensor_reduce(qm[:, j:j + 1], lam[:, 0:C], AX.X, Op.max,
                                                    apply_absolute_value=True),
                 r=["lam"], w=["qmj"])
        EM.O("ve", lambda: ve.tensor_reduce(qm[:, 8:9], qm[:, 0:NT], AX.X, Op.max),
             r=["qmj"], w=["qm8"])
        EM.O("te", lambda identv=identv: te.transpose(bankp(0, 0, 1, 0, 128), qm[:, 8:9],
                                                      identv),
             r=["qm8", f"x{islot}"], w=["bk0"])
        EM.O("ve", lambda: ve.tensor_reduce(qm[0:1, 9:10], bankp(0, 0, 1, 0, 128), AX.X,
                                            Op.max), r=["bk0"], w=["qm9"])
        EM.O("te", lambda: te.matmul(bankp(1, 0, 128, 0, 1), maskqg2[0:1, L:2 * L],
                                     qm[0:1, 9:10], start=True, stop=True),
             r=["maskqg", "qm9"], w=["bk1"])
        EM.O("ve", lambda: ve.tensor_copy(qm[:, 9:10], bankp(1, 0, 128, 0, 1)),
             r=["bk1"], w=["qm9"])
        EM.O("ve", lambda: ve.reciprocal(qm[:, 10:11], qm[:, 9:10]), r=["qm9"], w=["qm10"])
        EM.O("ve", lambda: ve.tensor_scalar(qm[:, 10:11], qm[:, 10:11], 7.0, None, Op.mult),
             r=["qm10"], w=["qm10"])
        EM.O("ve", lambda: ve.tensor_scalar(qm[:, 11:12], qm[:, 9:10], 1.0 / 7.0, None,
                                            Op.mult), r=["qm9"], w=["qm11"])
        for j in range(NT):
            EM.DQO(lambda j=j: sy.dma_start(out=lam[:, 0:C], in_=kblocks[j]), w=["lam"])
            EM.O("ve", lambda: ve.tensor_scalar(lam[:, 0:C], lam[:, 0:C], qm[:, 10:11], None,
                                                Op.mult), r=["lam", "qm10"], w=["lam"])
            EM.O("ve", lambda: ve.tensor_scalar(lam[:, 0:C], lam[:, 0:C], RND_MAGIC, RND_MAGIC,
                                                Op.add, Op.subtract), r=["lam"], w=["lam"])
            EM.O("ve", lambda: ve.tensor_scalar(lam[:, 0:C], lam[:, 0:C], 7.0, -8.0, Op.min,
                                                Op.max), r=["lam"], w=["lam"])
            EM.O("ve", lambda j=j: ve.tensor_scalar(ffnw[:, ts(j, C)], lam[:, 0:C],
                                                    qm[:, 11:12], None, Op.mult),
                 r=["lam", "qm11", "wr"], w=["wr"])

    def emit_ffn_front(c, EM):
        fp = c % 4
        xsrc = fxs[fp]
        xkey = fxk[fp]
        EM.DQO(lambda c=c, xsrc=xsrc: sy.dma_start(out=xsrc[:, :], in_=x2_rows[c]),
               r=[f"x2r{c}"], w=[xkey])
        ln_into(EM, f_xh, xsrc, xkey, "xh_tm", statsP, "statsP")
        pe_transpose(EM, f_xh, "xh_tm", f_xhch[fp], f_xk[fp])

    def emit_ffn_backM(c, EM):
        fp = c % 4
        fo = c % 2
        fok = f_ok[fo]
        def grp_lor(fp=fp):
            for ct in range(NT):
                ii = te.matmul(bankp(6, 0, 16, 0, L), ffna[:, ts(ct, 16)],
                               f_xhch[fp][:, ts(ct, L)], start=(ct == 0), stop=(ct == NT - 1))
            return ii
        EM.O("te", grp_lor, r=[f_xk[fp], "ffna"], w=["bk6"])
        EM.O("se", lambda: se.copy(f_lor[0:16, :], bankp(6, 0, 16, 0, L)),
             r=["bk6"], w=["f_lor2"])
        for half in range(2):
            b_i = 4 + half
            def grp_ffn(half=half, b_i=b_i, fp=fp):
                for ct in range(NT):
                    te.matmul(bank(b_i), f_xhch[fp][:, ts(ct, L)],
                              wsl(ffnw, ct, half * 512, 512),
                              start=(ct == 0), stop=False)
                return te.matmul(bank(b_i), f_lor[:, :], ffnb[:, ts(half, 512)],
                                 start=False, stop=True)
            EM.O("te", grp_ffn, r=[f_xk[fp], "wr", "f_lor", "f_lor2", "ffnb"], w=[f"bk{b_i}"])
            EM.O("se", lambda half=half, b_i=b_i, fo=fo: se.copy(
                f_outb[fo][:, ts(half, 512)], bank(b_i)),
                 r=[f"bk{b_i}"], w=[fok])

    def emit_ffn_backT(c, EM):
        fp = c % 4
        fo = c % 2
        of = c % 2
        xsrc = fxs[fp]
        xkey = fxk[fp]
        ok = o_tk[of]
        fok = f_ok[fo]
        ln_into(EM, f_outb[fo], f_outb[fo], fok, fok, statsQ, "statsQ")
        EM.O("ve", lambda xsrc=xsrc, of=of, fo=fo: ve.tensor_tensor(
            o_tmb[of][:, :], xsrc[:, :], f_outb[fo][:, :], Op.add),
             r=[xkey, fok], w=[ok])
        EM.DQO(lambda c=c, of=of: sy.dma_start(out=out_rows[c], in_=o_tmb[of][:, :]),
               r=[ok], w=[f"outr{of}"])

    with nc.Block() as block:
        for dst, src, b in [(mixv, D["mixv"], "mixv"), (kkv, D["kkv"], "kkv"),
                            (kav, D["kav"], "kav"), (maskqg2, D["maskqg2"], "maskqg"),
                            (xctxc, D["xctx_col"], "xctxc"),
                            (ss, D["s0_in"], "ss"),
                            (selbd, D["selbd"], "selbd")]:
            DQO(lambda dst=dst, src=src: sy.dma_start(out=dst[:, :], in_=src), w=[b])
        DQO(lambda: sy.dma_start(out=pt_sb[:, :], in_=d_par), w=["pt"])
        PT_SEM[0] = wl["pt"]
        blk = lambda ap: ap.rearrange("(j p) c -> p j c", p=128)
        def blkdst(t):
            return t.ap().rearrange("p (j c) -> p j c", j=NT)
        for dst, src, b in [(blkdst(wr), blk(D["W_r"]), "wr"), (blkdst(wk), blk(D["W_k"]), "wk"),
                            (blkdst(wv), blk(D["W_v"]), "wv"), (blkdst(wo), blk(D["W_o"]), "wo"),
                            (blkdst(w1), blk(D["w1"]), "w1"), (blkdst(a1), blk(D["a1"]), "a1"),
                            (blkdst(g1), blk(D["g1"]), "g1"), (blkdst(ffna), blk(D["ffn_a"]), "ffna"),
                            (w2a[:, :], D["w2aug"], "w2a"), (a2a[:, :], D["a2aug"], "a2a"),
                            (g2[:, :], D["g2"], "g2"), (ffnb[:, :], D["ffn_baug"], "ffnb"),
                            (rkbd[:, :], D["rkbd"], "rkbd"), (lnw[:, :], D["lnw_rep"], "lnw"),
                            (lnb[:, :], D["lnb_rep"], "lnb"), (bo2[:, :], D["bo2"], "bo2"),
                            (eyebf[:, :], D["ident"], "eye")]:
            O("gq", lambda dst=dst, src=src: gp_raw.dma_start(out=dst, in_=src), w=[b])
        O("ve", lambda: ve.memset(constt[:, 0:1], EPS_LN), w=["constt"])
        O("ve", lambda: ve.memset(constt[:, 1:2], EPS_GN), r=["constt"], w=["constt"])
        O("ve", lambda: ve.memset(constt[:, 2:3], NEG_EXP_HALF), r=["constt"], w=["constt"])
        O("ve", lambda: ve.memset(constt[:, 3:4], 1.0), r=["constt"], w=["constt"])
        O("ve", lambda: ve.memset(constt[:, 4:5], POS_EXP_HALF), r=["constt"], w=["constt"])
        O("ve", lambda: ve.memset(tanht[HD:HD + 1, :], 1.0), w=["tanht_one"])
        O("ve", lambda: ve.memset(a1t[HD:HD + 1, :], 1.0), w=["a1t_one"])
        O("ve", lambda: ve.memset(f_lor[:, :], 1.0), w=["f_lor"])
        O("ve", lambda: ve.tensor_scalar(kav1m[:, :], kav[:, :], -1.0, 1.0, Op.mult, Op.add),
          r=["kav"], w=["kav1m"])

        O("se", lambda: se.activation(lamL[0][:, :], xctxc[:, :], AF.Square),
          r=["xctxc"], w=["lamL0"])
        O("te", lambda: te.matmul(bankp(0, 0, 1, 0, NT), constt[:, 3:4], xctxc[:, :],
                                  start=True, stop=True), r=["constt", "xctxc"], w=["bk0"])
        O("te", lambda: te.matmul(bankp(0, 0, 1, NT, NT), constt[:, 3:4], lamL[0][:, :],
                                  start=True, stop=True), r=["constt", "lamL0"], w=["bk0"])
        O("ve", lambda: ve.tensor_reduce(qm[0:1, 6:7], bankp(0, 0, 1, 0, NT), AX.X, Op.add),
          r=["bk0"], w=["qm6"])
        O("ve", lambda: ve.tensor_reduce(qm[0:1, 5:6], bankp(0, 0, 1, NT, NT), AX.X, Op.add),
          r=["bk0"], w=["qm5"])
        O("te", lambda: te.matmul(bankp(1, 0, 128, 0, 2), maskqg2[0:1, L:2 * L], qm[0:1, 5:7],
                                  start=True, stop=True), r=["maskqg", "qm5", "qm6"], w=["bk1"])
        O("ve", lambda: ve.tensor_copy(qm[:, 5:7], bankp(1, 0, 128, 0, 2)),
          r=["bk1"], w=["qm5", "qm6"])
        O("ve", lambda: ve.tensor_scalar(qm[:, 6:7], qm[:, 6:7], 1.0 / C, None, Op.mult),
          r=["qm6"], w=["qm6"])
        O("ve", lambda: ve.tensor_scalar(qm[:, 5:6], qm[:, 5:6], 1.0 / C, None, Op.mult),
          r=["qm5"], w=["qm5"])
        O("ve", lambda: ve.scalar_tensor_tensor(qm[:, 7:8], qm[:, 6:7], qm[:, 6:7], qm[:, 5:6],
                                                Op.mult, Op.subtract),
          r=["qm5", "qm6"], w=["qm7"])
        O("ve", lambda: ve.tensor_scalar(qm[:, 7:8], qm[:, 7:8], -1.0, EPS_LN, Op.mult, Op.add),
          r=["qm7"], w=["qm7"])
        O("se", lambda: se.activation(qm[:, 7:8], qm[:, 7:8], AF.Sqrt), r=["qm7"], w=["qm7"])
        O("ve", lambda: ve.reciprocal(qm[:, 7:8], qm[:, 7:8]), r=["qm7"], w=["qm7"])
        O("ve", lambda: ve.scalar_tensor_tensor(qm[:, 4:5], qm[:, 6:7], -1.0, qm[:, 7:8],
                                                Op.mult, Op.mult), r=["qm6", "qm7"], w=["qm4"])
        O("ve", lambda: ve.tensor_scalar(xctx_hat[:, :], xctxc[:, :], qm[:, 7:8], qm[:, 4:5],
                                         Op.mult, Op.add), r=["xctxc", "qm7", "qm4"], w=["xctxh"])

        emit_prep(0, DIR)
        for c in range(KCH):
            streams = [record(emit_scan, c)]
            if c + 1 < KCH:
                streams.append(record(emit_prep, c + 1))
            if c >= 1:
                streams.append(record(emit_post, c - 1))
            if c == KCH - 1:
                streams.append(record(emit_quant, 0))
            merge(streams)
        emit_post(KCH - 1, DIR)

        with gp_raw.register("pr_g") as pr_g:
            gp_raw.wait_ge(SH["gq"], CN["gq"])
            gp_raw.wait_ge(SH[PT_SEM[0][0]], PT_SEM[0][1])
            gp_raw.dma_start(out=D["cc_in"], in_=D["s0_in"]).then_inc(SH["czq"], 16)
            gp_raw.wait_ge(SH["czq"], 16)
            gp_raw.reg_load(pr_g, pt_sb[0:1, 0:1])
            with gp_raw.If_eq(pr_g, 0):
                gp_raw.wait_ge(SH["se"], CN["se"])
                gp_raw.dma_start(out=D["cc_in"], in_=ss[:, :]).then_inc(SH["s0q"], 16)
                gp_raw.wait_ge(SH["s0q"], 16)
        gp_raw.collective_compute("AllGather", Op.bypass,
                                  replica_groups=[[0, 1], [2, 3], [4, 5], [6, 7]],
                                  ins=[D["cc_in"]], outs=[D["cc_out"]]).then_inc(SH["cc"], 1)
        CN["cc"] += 1

        DQO(lambda: sy.dma_start(out=d_send, in_=ss[:, :]), r=["ss"], w=["send"])


        emit_ffn_front(0, DIR)
        if KCH > 1:
            emit_ffn_front(1, DIR)
        for c in range(KCH):
            streams = [record(emit_ffn_backM, c)]
            if c >= 1:
                streams.append(record(emit_ffn_backT, c - 1))
            if c + 2 < KCH:
                streams.append(record(emit_ffn_front, c + 2))
            merge(streams)
        emit_ffn_backT(KCH - 1, DIR)

        for i in range(NDQ):
            s = f"dq{i}"
            if CN[s] > 0:
                sy.wait_ge(SH[s], CN[s])
        gp_raw.wait_ge(SH["gq"], CN["gq"])

    lp.__exit__(None, None, None)
    es.close()
    return nc


def make_core_inputs(inputs, SEG):
    g = lambda n: np.ascontiguousarray(inputs[n], dtype=np.float32)
    x = g("L0_attn_x")
    base = {}
    base["W_r"] = g("L0_attn_W_r"); base["W_k"] = g("L0_attn_W_k")
    base["W_v"] = g("L0_attn_W_v"); base["W_o"] = g("L0_attn_W_o")
    base["w1"] = g("L0_attn_w1")
    base["w2aug"] = np.ascontiguousarray(
        np.concatenate([g("L0_attn_w2"), g("L0_attn_w0")[None, :]], axis=0))
    base["a1"] = g("L0_attn_a1")
    base["a2aug"] = np.ascontiguousarray(
        np.concatenate([g("L0_attn_a2"), g("L0_attn_a0")[None, :]], axis=0))
    base["g1"] = g("L0_attn_g1"); base["g2"] = g("L0_attn_g2")
    mixv = np.zeros((128, 6 * NT), np.float32)
    for mi, nm in enumerate(["x_r", "x_w", "x_k", "x_v", "x_a", "x_g"]):
        mixv[:, mi * NT:(mi + 1) * NT] = g("L0_attn_" + nm).reshape(NT, 128).T
    base["mixv"] = mixv
    base["kkv"] = np.ascontiguousarray(g("L0_attn_k_k").reshape(NT, 128).T)
    base["kav"] = np.ascontiguousarray(g("L0_attn_k_a").reshape(NT, 128).T)
    rk = g("L0_attn_r_k")
    rkbd = np.zeros((128, 2 * NT), np.float32)
    for i in range(NT):
        rkbd[0:64, 2 * i] = rk[2 * i]
        rkbd[64:128, 2 * i + 1] = rk[2 * i + 1]
    base["rkbd"] = rkbd
    base["lnw_rep"] = np.broadcast_to(g("L0_attn_ln_w")[None, :], (128, C)).copy()
    base["lnb_rep"] = np.broadcast_to(g("L0_attn_ln_b")[None, :], (128, C)).copy()
    bo2 = np.zeros((128, 2), np.float32); bo2[0:64, 0] = 1; bo2[64:128, 1] = 1
    base["bo2"] = bo2
    selbd = np.zeros((2, 128), np.float32); selbd[0, 0:64] = 1; selbd[1, 64:128] = 1
    base["selbd"] = selbd
    base["ident"] = np.eye(128, dtype=np.float32)
    mq = np.triu(np.ones((L, L), np.float32), 1)
    mg = np.triu(np.ones((L, L), np.float32), 0)
    mp = np.tril(np.ones((L, L), np.float32), -1)
    base["maskqg2"] = np.ascontiguousarray(np.concatenate([mq, mg, mq, mg, mp], axis=1))
    base["ffn_kernel"] = g("L0_ffn_kernel")
    base["ffn_a"] = g("L0_ffn_a")
    base["ffn_baug"] = np.ascontiguousarray(
        np.concatenate([g("L0_ffn_b"), g("L0_ffn_bias")[None, :]], axis=0))

    maps = []
    B = x.shape[0]
    for core in range(2 * B):
        b, half = core // 2, core % 2
        m = dict(base)
        m["x_seg"] = np.ascontiguousarray(x[b, half * SEG:(half + 1) * SEG])
        if half == 0:
            m["xctx_col"] = np.zeros((128, NT), np.float32)
        else:
            m["xctx_col"] = np.ascontiguousarray(x[b, half * SEG - 1].reshape(NT, 128).T)
        m["s0_in"] = np.zeros((HD, H * HD), np.float32)
        m["par"] = np.array([[half]], np.int32)
        maps.append(m)
    return maps


def kernel(**inputs):
    from concourse.bass_utils import run_bass_kernel_spmd
    SEG = inputs["L0_attn_x"].shape[1] // 2
    maps = make_core_inputs(inputs, SEG)
    nc = build_nc(SEG)
    res = run_bass_kernel_spmd(nc, maps, core_ids=list(range(len(maps))))
    B, T, _ = inputs["L0_attn_x"].shape
    out = np.zeros((B, T, C), np.float32)
    for core in range(len(maps)):
        b, half = core // 2, core % 2
        out[b, half * SEG:(half + 1) * SEG] = res.results[core]["out_seg"]
    return out


# version 37
# speedup vs baseline: 1.0097x; 1.0097x over previous
import contextlib
import numpy as np
import concourse.bass as bass
import concourse.mybir as mybir
from concourse.alu_op_type import AluOpType as Op

F32 = mybir.dt.float32
BF16 = mybir.dt.bfloat16
I32 = mybir.dt.int32
AF = mybir.ActivationFunctionType
AX = mybir.AxisListType

C = 1024
H = 16
HD = 64
L = 128
NT = 8
EPS_LN = 1e-5
EPS_GN = 0.00064
NEG_EXP_HALF = -float(np.exp(-0.5))
POS_EXP_HALF = float(np.exp(-0.5))
RND_MAGIC = 12582912.0
NLEV = 7


def ts(i, n):
    return slice(i * n, (i + 1) * n)


def qb(h):
    return (h // 2) * 8 * L + (h % 2) * 4 * L


def xsl(h):
    lo = (h % 2) * 512 + (h // 2) * HD
    return slice(lo, lo + HD)


def wsl(wt, ct, fs, n):
    return wt[:, ct * C + fs: ct * C + fs + n]


def build_nc(SEG, debug=False):
    import os as _os
    NCH = SEG // L
    KCH = int(_os.environ.get('KCH', str(NCH)))
    nc = bass.Bass("TRN2", target_bir_lowering=False, debug=debug)

    D = {}
    def dram(name, shape, dtype=F32, kind=None):
        if kind:
            D[name] = nc.dram_tensor(name, shape, dtype, kind=kind).ap()
        else:
            D[name] = nc.dram_tensor(name, shape, dtype).ap()
        return D[name]

    d_x = dram("x_seg", [SEG, C], kind="ExternalInput")
    d_par = nc.dram_tensor("par", [1, 1], I32, kind="ExternalInput").ap()
    dram("cc_in", [HD, H * HD])
    dram("cc_out", [2 * HD, H * HD])
    for nm, shp in [("xctx_col", [128, NT]),
                    ("W_r", [C, C]), ("W_k", [C, C]), ("W_v", [C, C]), ("W_o", [C, C]),
                    ("w1", [C, HD]), ("w2aug", [HD + 1, C]), ("a1", [C, HD]),
                    ("a2aug", [HD + 1, C]), ("g1", [C, 128]), ("g2", [128, C]),
                    ("mixv", [128, 6 * NT]), ("kkv", [128, NT]), ("kav", [128, NT]),
                    ("rkbd", [128, 2 * NT]), ("lnw_rep", [128, C]), ("lnb_rep", [128, C]),
                    ("bo2", [128, 2]), ("selbd", [2, 128]), ("ident", [128, 128]),
                    ("maskqg2", [128, 5 * L]),
                    ("ffn_kernel", [C, C]), ("ffn_a", [C, 16]), ("ffn_baug", [17, C]),
                    ("s0_in", [HD, H * HD])]:
        dram(nm, shp, kind="ExternalInput")
    d_out = dram("out_seg", [SEG, C], kind="ExternalOutput")
    d_send = dram("s_end", [HD, H * HD], kind="ExternalOutput")
    d_x2 = dram("x2_spill", [SEG, C])

    x_rows = d_x.rearrange("(n p) c -> n p c", p=L)
    out_rows = d_out.rearrange("(n p) c -> n p c", p=L)
    x2_rows = d_x2.rearrange("(n p) c -> n p c", p=L)

    es = contextlib.ExitStack()
    def sb(name, shape, dtype, **kw):
        return es.enter_context(nc.sbuf_tensor(name, shape, dtype, **kw))

    wr = sb("wr", [128, NT * C], BF16); wk = sb("wk", [128, NT * C], BF16)
    wv = sb("wv", [128, NT * C], BF16); wo = sb("wo", [128, NT * C], BF16)
    w1 = sb("w1s", [128, NT * HD], BF16)
    a1 = sb("a1s", [128, NT * HD], BF16)
    g1 = sb("g1s", [128, NT * 128], BF16)
    ffna = sb("ffna_s", [128, NT * 16], BF16)
    w2a = sb("w2a", [HD + 1, C], BF16); a2a = sb("a2a", [HD + 1, C], BF16)
    g2 = sb("g2s", [128, C], BF16)
    ffnb = sb("ffnb_s", [17, C], BF16)
    mixv = sb("mixv_s", [128, 6 * NT], F32)
    kkv = sb("kkv_s", [128, NT], F32); kav = sb("kav_s", [128, NT], F32)
    kav1m = sb("kav1m", [128, NT], F32)
    rkbd = sb("rkbd_s", [128, 2 * NT], BF16)
    lnw = sb("lnw_s", [128, C], BF16); lnb = sb("lnb_s", [128, C], BF16)
    bo2 = sb("bo2_s", [128, 2], BF16); selbd = sb("selbd_s", [2, 128], F32)
    maskqg2 = sb("maskqg2_s", [128, 5 * L], F32)
    xctxc = sb("xctxc", [128, NT], F32)
    xctx_hat = sb("xctx_hat", [128, NT], BF16)
    ss = sb("ss_f", [HD, H * HD], F32)
    ssb = sb("ssb", [128, H * HD], BF16)
    x_tm = [sb(f"x_tm{i}", [L, C], F32) for i in range(3)]
    xh_tm = sb("xh_tm", [L, C], BF16)
    xh_ch = [sb(f"xh_ch{p}", [128, NT * L], BF16, is_transpose_dest=True) for p in range(2)]
    t1c = sb("t1c", [128, NT * L], BF16)
    mix = {m: sb(f"mix_{m}", [128, NT * L], BF16) for m in "rkv"}
    mixS = sb("mixS", [128, NT * L], BF16)
    statsP = sb("statsP", [L, 16], F32)
    statsQ = sb("statsQ", [L, 16], F32)
    rbf = sb("rbf", [128, NT * L], BF16, is_transpose_dest=True)
    kbf = sb("kbf", [128, NT * L], BF16)
    ktil = sb("ktil", [128, NT * L], BF16)
    k2bf = sb("k2bf", [128, NT * L], BF16)
    abf = sb("abf", [128, NT * L], BF16)
    sigw = sb("sigw", [128, NT * L], BF16)
    lam = sb("lam", [128, NT * L], F32)
    rdch = sb("rdch", [128, NT * L], F32)
    rlam = sb("rlam", [128, NT * L], BF16)
    rnorm = sb("rnorm", [2, NT * L], F32)
    tanht = sb("tanht", [HD + 1, L], BF16)
    a1t = sb("a1t", [HD + 1, L], BF16)
    sgt = sb("sgt", [128, L], BF16)
    AR1 = sb("AR1", [128, NT * 2 * L], BF16)
    Bh1 = sb("Bh1", [128, NT * L], BF16)
    Kh1 = sb("Kh1", [128, NT * L], BF16)
    Btm1 = sb("Btm1", [L, C], BF16, is_transpose_dest=True)
    Ktm1 = sb("Ktm1", [L, C], BF16, is_transpose_dest=True)
    vtm = [sb(f"vtm{i}", [L, C], BF16) for i in range(2)]
    gtm = [sb(f"gtm{i}", [L, C], BF16) for i in range(2)]
    lamL = [sb(f"lamL{i}", [128, NT], F32) for i in range(2)]
    coef = [sb(f"coef{i}", [L, H], F32) for i in range(2)]
    QNG = sb("QNG", [128, H * 4 * L], BF16)
    Ppr = sb("Ppr", [128, H * L], BF16)
    Qpr = sb("Qpr", [128, H * L], BF16)
    Xs = sb("Xs", [L, H * HD], BF16)
    y_tm = sb("y_tm", [L, C], F32)
    ysq = sb("ysq", [L, C], F32)
    gnst = sb("gnst", [L, 4 * H], F32)
    yg_tm = sb("yg_tm", [L, C], BF16)
    yg_ch = sb("yg_ch", [128, NT * L], BF16, is_transpose_dest=True)
    att_tm = sb("att_tm", [L, C], F32)
    f_lor = sb("f_lor", [17, L], BF16)
    qm = sb("qmax", [128, 12], F32)
    eyebf = sb("eyebf", [128, 128], BF16)
    constt = sb("constt", [128, 6], F32)
    pt_sb = sb("pt_sb", [1, 1], I32)
    pt2_sb = sb("pt2_sb", [1, 1], I32)
    f_xh = xh_tm
    f_outb = [att_tm, lam]
    f_ok = ["att", "lam"]
    fxs = [x_tm[0], x_tm[1], x_tm[2], rdch]
    fxk = ["x0", "x1", "x2", "rdch"]
    f_xhch = [yg_ch, rbf, kbf, ktil]
    f_xk = ["ygch", "rbf", "kbf", "ktil"]
    ffnw = wr
    o_tmb = [y_tm, ysq]
    o_tk = ["y", "ysq"]
    nc.const_aps.register(F32, EPS_LN, constt[:, 0:1])
    nc.const_aps.register(F32, EPS_GN, constt[:, 1:2])
    nc.const_aps.register(F32, NEG_EXP_HALF, constt[:, 2:3])
    nc.const_aps.register(F32, POS_EXP_HALF, constt[:, 4:5])

    pbank = es.enter_context(nc.psum_tensor("pbank", [128, 8 * 512], F32))

    def bank(i, lo=0, n=512):
        return pbank[:, i * 512 + lo: i * 512 + lo + n]

    def bankp(i, p0, np_, lo=0, n=512):
        return pbank[p0:p0 + np_, i * 512 + lo: i * 512 + lo + n]

    def bankspan(i, n2):
        return pbank[:, i * 512: i * 512 + n2]

    NDQ = 12
    sem_names = (["sq", "gq", "te", "ve", "se", "ge", "cc", "s0q", "s0ready", "czq", "tq",
                  "ccin"]
                 + [f"dq{i}" for i in range(NDQ)])
    SH = {n: es.enter_context(nc.semaphore(n)) for n in sem_names}
    CN = {n: 0 for n in sem_names}

    _COMPUTE_OPS = {"tensor_tensor", "tensor_scalar", "scalar_tensor_tensor", "tensor_copy",
                    "tensor_reduce", "reduce_sum", "reduce_max", "reciprocal", "memset",
                    "bn_stats", "bn_aggr", "tensor_tensor_scan", "activation", "copy",
                    "square", "sqrt", "tensor_add", "tensor_mul", "tensor_sub",
                    "tensor_scalar_mul", "tensor_scalar_add", "select", "memzero",
                    "add", "mul"}

    sy = nc.sync
    gp_raw = nc.gpsimd
    te = nc.tensor
    ve = nc.vector
    se = nc.scalar
    gp = nc.gpsimd

    E = {"te": (te, "te", 1), "ve": (ve, "ve", 1), "se": (se, "se", 1),
         "ge": (gp, "ge", 1), "sq": (sy, "sq", 16), "gq": (gp_raw, "gq", 16)}

    wl = {}
    rl = {}
    waited = {k: {} for k in E}

    def _wait(ek, sem, cnt):
        eng = E[ek][0]
        own = E[ek][1]
        if sem == own and ek == "te":
            return
        if sem == "gq":
            cnt = CN[sem]
        if waited[ek].get(sem, -1) >= cnt:
            return
        eng.wait_ge(SH[sem], cnt)
        waited[ek][sem] = cnt

    def O(ek, fn, r=(), w=(), group=False):
        eng, sem, amt = E[ek]
        for b in w:
            if b in wl:
                _wait(ek, *wl[b])
            for s2, c2 in rl.get(b, {}).items():
                _wait(ek, s2, c2)
        for b in r:
            if b in wl:
                _wait(ek, *wl[b])
        instr = fn()
        CN[sem] += amt
        instr.then_inc(SH[sem], amt)
        for b in w:
            wl[b] = (sem, CN[sem])
            rl[b] = {}
        for b in r:
            rl.setdefault(b, {})[sem] = CN[sem]
        return instr

    dq_state = {"i": 0}

    def DQO(fn, r=(), w=(), tp=False):
        if tp:
            s = "tq"
        else:
            s = f"dq{dq_state['i'] % NDQ}"
            dq_state["i"] += 1
        if CN[s] > 0:
            _wait_raw(sy, "syeng", s, CN[s])
        for b in w:
            if b in wl:
                _wait_raw(sy, "syeng", *wl[b])
            for s2, c2 in rl.get(b, {}).items():
                _wait_raw(sy, "syeng", s2, c2)
        for b in r:
            if b in wl:
                _wait_raw(sy, "syeng", *wl[b])
        instr = fn()
        CN[s] += 16
        instr.then_inc(SH[s], 16)
        for b in w:
            wl[b] = (s, CN[s])
            rl[b] = {}
        for b in r:
            rl.setdefault(b, {})[s] = CN[s]
        return instr

    def _wait_raw(eng, ek, sem, cnt):
        if sem == "gq":
            cnt = CN[sem]
        if waited.setdefault(ek, {}).get(sem, -1) >= cnt:
            return
        eng.wait_ge(SH[sem], cnt)
        waited[ek][sem] = cnt

    class DirectEM:
        direct = True

        def O(self, *a, **k):
            k.pop("presplit", False)
            return O(*a, **k)

        def DQO(self, *a, **k):
            return DQO(*a, **k)

        def cut(self):
            pass

        def mark(self, name):
            pass

        def hold(self, name):
            pass

    class Rec:
        direct = False

        def __init__(self):
            self.atoms = []
            self.cur = []
            self.pending = set()
            self.marks = {}
            self.holds = {}

        def mark(self, name):
            self.cut()
            self.marks[name] = len(self.atoms)

        def hold(self, name):
            self.cut()
            self.holds[len(self.atoms)] = name

        def _append(self, it, k):
            if not self.pending:
                self.cut()
            self.cur.append(it)
            for b in k.get("r", ()):
                self.pending.discard(b)
            for b in k.get("w", ()):
                if b.startswith("bk"):
                    self.pending.add(b)

        def O(self, *a, **k):
            if k.pop("presplit", False):
                self.pending.clear()
                self.cut()
            self._append(("O", a, k), k)

        def DQO(self, *a, **k):
            self._append(("D", a, k), k)

        def cut(self):
            if self.cur:
                self.atoms.append(self.cur)
                self.cur = []

    DIR = DirectEM()

    def record(fn, c):
        r = Rec()
        fn(c, r)
        r.cut()
        return r

    def replay(item):
        t, a, k = item
        if t == "O":
            O(*a, **k)
        else:
            DQO(*a, **k)

    def merge(streams):
        idx = [0] * len(streams)
        marks = streams[0].marks
        live = True
        while live:
            live = False
            for i, s in enumerate(streams):
                if idx[i] < len(s.atoms):
                    nm = s.holds.get(idx[i])
                    if nm is not None and i != 0 and idx[0] < marks.get(nm, 0):
                        live = True
                        continue
                    for item in s.atoms[idx[i]]:
                        replay(item)
                    idx[i] += 1
                    live = True

    lp = nc.allow_low_precision(reason="bf16 matmul operands; fp32 accumulation kept")
    lp.__enter__()

    def ln_into(EM, dst_bf, src_f32, src_key, dst_key, st, stk, scr=None, sck=None):
        if scr is None:
            EM.O("ve", lambda: ve.bn_stats(st[:, 0:6], src_f32[:, 0:512]),
                 r=[src_key], w=[stk])
            EM.O("ve", lambda: ve.bn_stats(st[:, 6:12], src_f32[:, 512:1024]),
                 r=[src_key, stk], w=[stk])
            EM.O("ve", lambda: ve.bn_aggr(st[:, 12:14], st[:, 0:12]),
                 r=[stk], w=[stk])
        else:
            EM.O("se", lambda: se.activation(scr[:, :], src_f32[:, :], AF.Identity,
                                             accum_out=st[:, 12:13]),
                 r=[src_key], w=[stk, sck])
            EM.O("se", lambda: se.activation(scr[:, :], src_f32[:, :], AF.Square,
                                             accum_out=st[:, 13:14]),
                 r=[src_key, stk], w=[stk, sck])
            EM.O("ve", lambda: ve.tensor_scalar(st[:, 12:14], st[:, 12:14], 1.0 / C, None,
                                                Op.mult), r=[stk], w=[stk])
            EM.O("ve", lambda: ve.tensor_tensor(st[:, 15:16], st[:, 12:13], st[:, 12:13],
                                                Op.mult), r=[stk], w=[stk])
            EM.O("ve", lambda: ve.tensor_tensor(st[:, 13:14], st[:, 13:14], st[:, 15:16],
                                                Op.subtract), r=[stk], w=[stk])
        EM.O("se", lambda: se.activation(st[:, 14:15], st[:, 13:14], AF.Sqrt,
                                         bias=EPS_LN), r=[stk, "constt"], w=[stk])
        EM.O("ve", lambda: ve.reciprocal(st[:, 14:15], st[:, 14:15]), r=[stk], w=[stk])
        EM.O("ve", lambda: ve.scalar_tensor_tensor(st[:, 15:16], st[:, 12:13], -1.0,
                                                   st[:, 14:15], Op.mult, Op.mult),
             r=[stk], w=[stk])
        EM.O("se", lambda: se.activation(dst_bf[:, :], src_f32[:, :], AF.Identity,
                                         bias=st[:, 15:16], scale=st[:, 14:15]),
             r=[src_key, stk], w=[dst_key])

    PT_SEM = [None]

    def pe_transpose(EM, src, src_key, dst, dst_key, extra_r=(), b0=2):
        for bq in range(2):
            def grp_tx(bq=bq, src=src):
                for f in range(4):
                    ct = bq * 4 + f
                    ii = te.matmul(bank(b0 + bq, f * L, L), src[:, ts(ct, 128)],
                                   eyebf[:, :], start=True, stop=True)
                return ii
            EM.O("te", grp_tx, r=[src_key, "eye"] + list(extra_r), w=[f"bk{b0 + bq}"])
            EM.O("se", lambda bq=bq, dst=dst: se.copy(dst[:, ts(bq, 512)], bank(b0 + bq)),
                 r=[f"bk{b0 + bq}"], w=[dst_key])

    def emit_prep(c, EM):
        pr2, pr3 = c % 2, c % 3
        xk = f"xh_ch{pr2}"
        xkp = f"xh_ch{1 - pr2}"
        ark = "AR"
        AR = AR1
        xkey = f"x{pr3}"
        vk = f"vtm{pr2}"
        gk = f"gtm{pr2}"
        ck = f"coef{pr2}"
        lk = f"lamL{pr2}"
        EM.DQO(lambda c=c, pr3=pr3: sy.dma_start(out=x_tm[pr3][:, :], in_=x_rows[c]),
               w=[xkey])
        ln_into(EM, xh_tm, x_tm[pr3], xkey, "xh_tm", statsP, "statsP",
                scr=t1c, sck="t1c")
        pe_transpose(EM, xh_tm, "xh_tm", xh_ch[pr2], xk, b0=6)
        if c == 0:
            assert EM.direct
            with sy.register("ps_r") as ps_r:
                sy.wait_ge(SH[PT_SEM[0][0]], PT_SEM[0][1])
                ss_sem = wl["ss"]
                sy.wait_ge(SH[ss_sem[0]], ss_sem[1])
                sy.reg_load(ps_r, pt_sb[0:1, 0:1])
                with sy.If_eq(ps_r, 1):
                    sy.wait_ge(SH["cc"], 1)
                    sy.dma_start(out=ss[:, :], in_=D["cc_out"][0:HD, :]).then_inc(SH["s0q"], 16)
                    sy.wait_ge(SH["s0q"], 16)
                sy.dma_start(out=pt2_sb[:, :], in_=d_par).then_inc(SH["s0ready"], 16)
            CN["s0ready"] += 16
            wl["ss"] = ("s0ready", 16)
        t1v = t1c.ap().rearrange("p (i l) -> p i l", l=L)
        xhv = xh_ch[pr2].ap().rearrange("p (i l) -> p i l", l=L)
        EM.O("ve", lambda t1v=t1v, xhv=xhv: ve.tensor_tensor(
            t1v[:, :, 1:L], xhv[:, :, 0:L - 1], xhv[:, :, 1:L], Op.subtract),
            r=[xk], w=["t1c"])
        if c == 0:
            prev_col = xctx_hat.ap().rearrange("p (i one) -> p i one", one=1)
            pkey = "xctxh"
        else:
            prev_col = xh_ch[1 - pr2].ap().rearrange("p (i l) -> p i l", l=L)[:, :, L - 1:L]
            pkey = xkp
        EM.O("ve", lambda t1v=t1v, xhv=xhv, prev_col=prev_col: ve.tensor_tensor(
            t1v[:, :, 0:1], prev_col, xhv[:, :, 0:1], Op.subtract),
            r=[xk, pkey, "t1c"], w=["t1c"])

        t3 = t1c.ap().rearrange("p (i l) -> p i l", l=L)
        x3 = xh_ch[pr2].ap().rearrange("p (i l) -> p i l", l=L)

        def emit_mix(dst, dk, mi, extra=()):
            d3 = dst.ap().rearrange("p (i l) -> p i l", l=L)
            m3 = mixv.ap()[:, mi * NT:(mi + 1) * NT].rearrange(
                "p (i one) -> p i one", one=1).broadcast_to((128, NT, L))
            EM.O("ve", lambda d3=d3, m3=m3: ve.tensor_tensor(d3, t3, m3, Op.mult),
                 r=["t1c", "mixv"] + list(extra), w=[dk])
            EM.O("ve", lambda d3=d3: ve.tensor_tensor(d3, d3, x3, Op.add),
                 r=[xk, dk], w=[dk])

        emit_mix(mixS, "mxS", 1)
        def grp_w1():
            for ct in range(NT):
                ii = te.matmul(bankp(6, 0, HD, 0, L), w1[:, ts(ct, HD)],
                               mixS[:, ts(ct, L)], start=(ct == 0), stop=(ct == NT - 1))
            return ii
        EM.O("te", grp_w1, r=["mxS", "w1"], w=["bk6"])
        EM.O("se", lambda: se.activation(tanht[0:HD, :], bankp(6, 0, HD, 0, L), AF.Tanh),
             r=["bk6"], w=["tanht"])
        emit_mix(mixS, "mxS", 4)
        def grp_a1():
            for ct in range(NT):
                ii = te.matmul(bankp(6, 0, HD, L, L), a1[:, ts(ct, HD)],
                               mixS[:, ts(ct, L)], start=(ct == 0), stop=(ct == NT - 1))
            return ii
        EM.O("te", grp_a1, r=["mxS", "a1"], w=["bk6"])
        EM.O("se", lambda: se.copy(a1t[0:HD, :], bankp(6, 0, HD, L, L)),
             r=["bk6"], w=["a1t"])
        emit_mix(mixS, "mxS", 5)
        def grp_g1():
            for ct in range(NT):
                ii = te.matmul(bank(6, 2 * L, L), g1[:, ts(ct, 128)],
                               mixS[:, ts(ct, L)], start=(ct == 0), stop=(ct == NT - 1))
            return ii
        EM.O("te", grp_g1, r=["mxS", "g1"], w=["bk6"])
        EM.O("se", lambda: se.activation(sgt[:, :], bank(6, 2 * L, L), AF.Sigmoid),
             r=["bk6"], w=["sgt"])
        for g4 in range(2):
            def grp_z(g4=g4):
                for f4 in range(4):
                    ft = g4 * 4 + f4
                    te.matmul(bank(7, f4 * L, L), w2a[:, ts(ft, 128)], tanht[:, :],
                              start=True, stop=True)
                    ii = te.matmul(bank(6, f4 * L, L), a2a[:, ts(ft, 128)], a1t[:, :],
                                   start=True, stop=True)
                return ii
            EM.O("te", grp_z, r=["tanht", "tanht_one", "w2a", "a1t", "a1t_one", "a2a"],
                 w=["bk7", "bk6"])
            EM.O("se", lambda g4=g4: se.activation(sigw[:, ts(g4, 512)], bank(7),
                                                   AF.Sigmoid), r=["bk7"], w=["sigw"])
            EM.O("se", lambda g4=g4: se.activation(abf[:, ts(g4, 512)], bank(6),
                                                   AF.Sigmoid), r=["bk6"], w=["abf"])
        for half in range(2):
            EM.O("te", lambda half=half: te.matmul(bank(7), sgt[:, :], g2[:, ts(half, 512)],
                                                   start=True, stop=True),
                 r=["sgt", "g2"], w=["bk7"])
            EM.O("se", lambda half=half, pr2=pr2: se.copy(gtm[pr2][:, ts(half, 512)],
                                                          bank(7)),
                 r=["bk7"], w=[gk])
        EM.O("se", lambda: se.activation(ysq[:, :], sigw[:, :], AF.Exp, scale=NEG_EXP_HALF),
             r=["sigw", "constt"], w=["ysq"])
        dv = ysq.ap().rearrange("p (i l) -> p i l", l=L)
        for i_t in range(NT):
            EM.O("ve", lambda i_t=i_t, dv=dv: ve.tensor_tensor_scan(lam[:, ts(i_t, L)],
                                                                    dv[:, i_t], dv[:, i_t], 1.0,
                                                                    Op.mult, Op.bypass),
                 r=["ysq"], w=["lam"])
        EM.O("se", lambda: se.activation(rdch[:, :], sigw[:, :], AF.Exp, scale=POS_EXP_HALF),
             r=["sigw", "constt"], w=["rdch"])
        dv2 = rdch.ap().rearrange("p (i l) -> p i l", l=L)
        for i_t in range(NT):
            EM.O("ve", lambda i_t=i_t, dv2=dv2: ve.tensor_tensor_scan(rlam[:, ts(i_t, L)],
                                                                      dv2[:, i_t], dv2[:, i_t],
                                                                      1.0, Op.mult, Op.bypass),
                 r=["rdch"], w=["rlam"])
        lam_last = lam.ap().rearrange("p (i l) -> p i l", l=L)[:, :, L - 1:L]
        EM.O("ve", lambda lam_last=lam_last, pr2=pr2: ve.tensor_copy(
            lamL[pr2].ap().rearrange("p (i one) -> p i one", one=1), lam_last),
            r=["lam"], w=[lk])
        emit_mix(mix["k"], "mxk", 2)
        for bk2 in range(2):
            b_i = 6 + bk2
            def grpk(b_i=b_i, bk2=bk2):
                for f4 in range(4):
                    ft = bk2 * 4 + f4
                    for ct in range(NT):
                        ii = te.matmul(bank(b_i, f4 * L, L), wsl(wk, ct, ft * 128, 128),
                                       mix["k"][:, ts(ct, L)],
                                       start=(ct == 0), stop=(ct == NT - 1))
                return ii
            EM.O("te", grpk, r=["mxk", "wk"], w=[f"bk{b_i}"])
            EM.O("se", lambda bk2=bk2, b_i=b_i: se.copy(kbf[:, ts(bk2, 4 * L)], bank(b_i)),
                 r=[f"bk{b_i}"], w=["kbf"])
        emit_mix(mix["r"], "mxr", 0)
        for bk2 in range(2):
            b_i = 6 + bk2
            def grpr(b_i=b_i, bk2=bk2):
                for f4 in range(4):
                    ft = bk2 * 4 + f4
                    for ct in range(NT):
                        ii = te.matmul(bank(b_i, f4 * L, L), wsl(wr, ct, ft * 128, 128),
                                       mix["r"][:, ts(ct, L)],
                                       start=(ct == 0), stop=(ct == NT - 1))
                return ii
            EM.O("te", grpr, r=["mxr", "wr"], w=[f"bk{b_i}"])
            EM.O("se", lambda bk2=bk2, b_i=b_i: se.copy(rbf[:, ts(bk2, 4 * L)], bank(b_i)),
                 r=[f"bk{b_i}"], w=["rbf"])
        emit_mix(mix["v"], "mxv", 3)
        for half in range(2):
            def grpv(half=half):
                for ct in range(NT):
                    ii = te.matmul(bank(6 + half), mix["v"][:, ts(ct, L)],
                                   wsl(wv, ct, half * 512, 512),
                                   start=(ct == 0), stop=(ct == NT - 1))
                return ii
            EM.O("te", grpv, r=["mxv", "wv"], w=[f"bk{6 + half}"])
            EM.O("se", lambda half=half, pr2=pr2: se.copy(vtm[pr2][:, ts(half, 512)],
                                                          bank(6 + half)),
                 r=[f"bk{6 + half}"], w=[vk])
        kt3 = ktil.ap().rearrange("p (i l) -> p i l", l=L)
        kb3 = kbf.ap().rearrange("p (i l) -> p i l", l=L)
        kkv3 = kkv.ap().rearrange("p (i one) -> p i one", one=1).broadcast_to((128, NT, L))
        EM.O("ve", lambda: ve.tensor_tensor(kt3, kb3, kkv3, Op.mult),
             r=["kbf", "kkv"], w=["ktil"])
        EM.O("ve", lambda: ve.tensor_tensor(sigw[:, :], ktil[:, :], ktil[:, :], Op.mult),
             r=["ktil"], w=["sigw"])
        for i_t in range(NT):
            b_i = 6 + (i_t % 2)
            EM.O("te", lambda i_t=i_t, b_i=b_i: te.matmul(
                bankp(b_i, 0, 2, (i_t // 2) * L, L), bo2[:, :], sigw[:, ts(i_t, L)],
                start=True, stop=True),
                r=["sigw", "bo2"], w=[f"bk{b_i}"])
        for b_i in (6, 7):
            EM.O("se", lambda b_i=b_i: se.activation(
                rnorm[:, (b_i - 6) * 4 * L:(b_i - 6) * 4 * L + 4 * L],
                bankp(b_i, 0, 2, 0, 4 * L), AF.Sqrt),
                r=[f"bk{b_i}"], w=["rnorm"])
        EM.O("ve", lambda: ve.tensor_scalar(rnorm[:, :], rnorm[:, :], 1e-12, None, Op.max),
             r=["rnorm"], w=["rnorm"])
        EM.O("ve", lambda: ve.reciprocal(rnorm[:, :], rnorm[:, :]),
             r=["rnorm"], w=["rnorm"])
        for i_t in range(NT):
            b_i = 6 + (i_t % 2)
            pos = (i_t % 2) * 4 + i_t // 2
            EM.O("te", lambda i_t=i_t, b_i=b_i, pos=pos: te.matmul(bank(b_i, 0, L), selbd[:, :],
                                                                   rnorm[:, ts(pos, L)],
                                                                   start=True, stop=True),
                 r=["rnorm", "selbd"], w=[f"bk{b_i}"])
            EM.O("ve", lambda i_t=i_t, b_i=b_i: ve.tensor_tensor(ktil[:, ts(i_t, L)],
                                                                 bank(b_i, 0, L),
                                                                 ktil[:, ts(i_t, L)], Op.mult),
                 r=[f"bk{b_i}"], w=["ktil"])
        k23 = k2bf.ap().rearrange("p (i l) -> p i l", l=L)
        ab3 = abf.ap().rearrange("p (i l) -> p i l", l=L)
        kav3 = kav.ap().rearrange("p (i one) -> p i one", one=1).broadcast_to((128, NT, L))
        kav1m3 = kav1m.ap().rearrange("p (i one) -> p i one", one=1).broadcast_to((128, NT, L))
        EM.O("ve", lambda: ve.tensor_tensor(k23, ab3, kav3, Op.mult),
             r=["abf", "kav"], w=["k2bf"])
        EM.O("ve", lambda: ve.tensor_tensor(k23, k23, kav1m3, Op.add),
             r=["kav1m", "k2bf"], w=["k2bf"])
        EM.O("ve", lambda: ve.tensor_tensor(k2bf[:, :], k2bf[:, :], kbf[:, :], Op.mult),
             r=["k2bf", "kbf"], w=["k2bf"])
        EM.O("ve", lambda: ve.tensor_tensor(sigw[:, :], rbf[:, :], k2bf[:, :], Op.mult),
             r=["rbf", "k2bf"], w=["sigw"])
        def grp_coef():
            for i_t in range(NT):
                ii = te.matmul(bank(6, i_t * 2, 2), sigw[:, ts(i_t, L)],
                               rkbd[:, 2 * i_t:2 * i_t + 2], start=True, stop=True)
            return ii
        EM.O("te", grp_coef, r=["sigw", "rkbd"], w=["bk6"])
        EM.O("ve", lambda pr2=pr2: ve.tensor_copy(coef[pr2][:, :], bank(6, 0, 2 * NT)),
             r=["bk6"], w=[ck])
        EM.hold("yred")
        ar_v = AR.ap().rearrange("p (i l2) -> p i l2", l2=2 * L)
        lam_v = lam.ap().rearrange("p (i l) -> p i l", l=L)
        kk_v = ktil.ap().rearrange("p (i l) -> p i l", l=L)
        EM.O("ve", lambda ar_v=ar_v, lam_v=lam_v, kk_v=kk_v: ve.scalar_tensor_tensor(
            ar_v[:, :, 1:L], lam_v[:, :, 0:L - 1], -1.0, kk_v[:, :, 1:L], Op.mult, Op.mult),
            r=["lam", "ktil"], w=[ark])
        EM.O("ve", lambda ar_v=ar_v, kk_v=kk_v: ve.scalar_tensor_tensor(
            ar_v[:, :, 0:1], kk_v[:, :, 0:1], -1.0, kk_v[:, :, 0:1], Op.mult, Op.bypass),
            r=["ktil", ark], w=[ark])
        r_v = rbf.ap().rearrange("p (i l) -> p i l", l=L)
        EM.O("ve", lambda ar_v=ar_v, r_v=r_v, lam_v=lam_v: ve.tensor_tensor(
            ar_v[:, :, L:2 * L], r_v, lam_v, Op.mult),
            r=["rbf", "lam", ark], w=[ark])
        EM.hold("gram")
        EM.O("ve", lambda: ve.tensor_tensor(Bh1[:, :], ktil[:, :], abf[:, :], Op.mult),
             r=["ktil", "abf"], w=["Bh"])
        EM.O("ve", lambda: ve.tensor_tensor(Bh1[:, :], Bh1[:, :], rlam[:, :], Op.mult),
             r=["Bh", "rlam"], w=["Bh"])
        EM.O("ve", lambda: ve.tensor_tensor(Kh1[:, :], k2bf[:, :], rlam[:, :], Op.mult),
             r=["k2bf", "rlam"], w=["Kh"])
        EM.hold("bkt")
        pe_transpose(EM, Bh1, "Bh", Btm1, "Btm", b0=6)
        pe_transpose(EM, Kh1, "Kh", Ktm1, "Ktm", b0=6)

    def emit_scan(c, EM):
        pr2 = c % 2
        SPL = (c != KCH - 1) and bool(int(_os.environ.get('KSPL', '0')) )
        KSPLM = int(_os.environ.get('KSPL', '0'))
        ark = "AR"
        AR = AR1
        vk = f"vtm{pr2}"
        lk = f"lamL{pr2}"
        vt = vtm[pr2]
        if c == 0:
            EM.O("se", lambda: se.copy(ssb[0:HD, :], ss[:, :]), r=["ss"], w=["ssb"])
            EM.O("se", lambda: se.copy(ssb[HD:128, :], ss[:, :]), r=["ss", "ssb"], w=["ssb"])
        for hp in range(H // 2):
            i_t = hp
            Ab = 0 if hp % 2 == 0 else 4
            for ph in range(2):
                prt = slice(ph * HD, ph * HD + HD)
                Asl = AR[prt, i_t * 2 * L: i_t * 2 * L + L]
                ARsl = AR[prt, i_t * 2 * L: (i_t + 1) * 2 * L]
                Bsl = Bh1[prt, ts(i_t, L)]
                Ksl = Kh1[prt, ts(i_t, L)]
                EM.O("te", lambda Bsl=Bsl, ARsl=ARsl, ph=ph, Ab=Ab:
                     te.matmul(bank(Ab + ph, 0, 2 * L), Bsl, ARsl, start=True, stop=True),
                     r=["Bh", ark], w=[f"bk{Ab + ph}"])
                EM.O("te", lambda Ksl=Ksl, ARsl=ARsl, ph=ph, Ab=Ab:
                     te.matmul(bank(Ab + ph, 2 * L, 2 * L), Ksl, ARsl, start=True, stop=True),
                     r=["Kh", ark], w=[f"bk{Ab + ph}"])
                EM.O("te", lambda Asl=Asl, Bsl=Bsl, ph=ph:
                     te.matmul(bank(2 + ph, 0, L), Asl, Bsl, start=True, stop=True),
                     r=[ark, "Bh"], w=[f"bk{2 + ph}"])
            pq = pbank.ap()[:, Ab * 512:(Ab + 2) * 512].rearrange(
                "p (b x) -> p b x", x=512)
            EM.O("ve", lambda hp=hp, pq=pq, Ab=Ab: ve.tensor_tensor(
                QNG.ap()[:, hp * 8 * L:(hp + 1) * 8 * L].rearrange("p (b x) -> p b x", x=4 * L),
                pq, maskqg2.ap()[:, 0:4 * L].rearrange(
                    "p (one x) -> p one x", one=1).broadcast_to((128, 2, 4 * L)), Op.mult),
                r=[f"bk{Ab}", f"bk{Ab + 1}", "maskqg"], w=["QNG"],
                presplit=SPL and bool(KSPLM & 1))
            pq3 = pbank.ap()[:, 2 * 512:4 * 512].rearrange("p (b x) -> p b x", x=512)[:, :, 0:L]
            EM.O("ve", lambda hp=hp, pq3=pq3: ve.tensor_tensor(
                Ppr.ap()[:, hp * 2 * L:(hp + 1) * 2 * L].rearrange("p (b x) -> p b x", x=L),
                pq3, maskqg2.ap()[:, 4 * L:5 * L].rearrange(
                    "p (one l) -> p one l", one=1).broadcast_to((128, 2, L)), Op.mult),
                r=["bk2", "bk3", "maskqg"], w=["Ppr"])
        EM.mark("gram")
        for h in range(H):
            b_i = 4 + h % 2
            bknm = f"bk{b_i}"
            prt = slice((h % 2) * HD, (h % 2) * HD + HD)
            i_t = h // 2
            Asl = AR[prt, i_t * 2 * L: i_t * 2 * L + L]
            def grp_rt(h=h, b_i=b_i, Asl=Asl, prt=prt):
                te.matmul(bank(b_i, (h // 2) * HD, HD), Asl, ssb[prt, ts(h, HD)],
                          start=True, stop=False)
                return te.matmul(bank(b_i, (h // 2) * HD, HD),
                                 QNG[:, qb(h) + 2 * L: qb(h) + 3 * L],
                                 vt[:, ts(h, HD)], start=False, stop=True)
            EM.O("te", grp_rt, r=[ark, "ssb", "QNG", vk], w=[bknm], group=True)
        EM.O("se", lambda: se.copy(Xs[:, 0:1024], bankspan(4, 1024)),
             r=["bk4", "bk5"], w=["Xs"])
        QvA = lambda h: QNG[:, qb(h): qb(h) + L]
        PvB = lambda h: QNG[:, qb(h) + 2 * L: qb(h) + 3 * L]
        for j in range(NLEV):
            even = (j % 2 == 0)
            lastsq = (j == NLEV - 2)
            lastX = (j == NLEV - 1)
            qk = "QNG" if even else "Qpr"
            pk = "Ppr" if even else "QNG"
            Qsrc = QvA if even else (lambda h: Qpr[:, ts(h, L)])
            Psrc = (lambda h: Ppr[:, ts(h, L)]) if even else PvB
            for par_h in range(2):
                b_i = 4 + par_h
                def grp_x(par_h=par_h, b_i=b_i, Qsrc=Qsrc, lastX=lastX):
                    for k8 in range(8):
                        h = 2 * k8 + par_h
                        if lastX:
                            ii = te.matmul(bankp(b_i, 0, HD, k8 * HD, HD),
                                           Qsrc(h)[:, 0:HD], Xs[:, xsl(h)],
                                           start=True, stop=True)
                        else:
                            ii = te.matmul(bank(b_i, k8 * HD, HD), Qsrc(h), Xs[:, xsl(h)],
                                           start=True, stop=True)
                    return ii
                EM.O("te", grp_x, r=[qk, "Xs"], w=[f"bk{b_i}"])
            if lastX:
                EM.O("ve", lambda: ve.tensor_tensor(Xs[0:HD, 0:1024],
                                                    pbank[0:HD, 4 * 512:4 * 512 + 1024],
                                                    Xs[0:HD, 0:1024], Op.add),
                     r=["bk4", "bk5", "Xs"], w=["Xs"], presplit=SPL and bool(KSPLM & 2))
            else:
                EM.O("ve", lambda: ve.tensor_tensor(Xs[:, 0:1024], bankspan(4, 1024),
                                                    Xs[:, 0:1024], Op.add),
                     r=["bk4", "bk5", "Xs"], w=["Xs"], presplit=SPL and bool(KSPLM & 2))
            if j < NLEV - 1:
                pok = "QNG" if even else "Ppr"
                qok = "Qpr" if even else "QNG"
                for gg in range(2):
                    for grpi in (2 * gg, 2 * gg + 1):
                        b_p = grpi % 2
                        b_q = 2 + grpi % 2
                        def grp_sq(grpi=grpi, b_p=b_p, b_q=b_q, Qsrc=Qsrc, Psrc=Psrc,
                                   lastsq=lastsq):
                            for k4 in range(4):
                                h = 4 * grpi + k4
                                if not lastsq:
                                    te.matmul(bank(b_p, k4 * L, L), Qsrc(h), Psrc(h),
                                              start=True, stop=True)
                                    ii = te.matmul(bank(b_q, k4 * L, L), Psrc(h), Qsrc(h),
                                                   start=True, stop=True)
                                else:
                                    ii = te.matmul(bank(b_q, k4 * L, HD), Psrc(h),
                                                   Qsrc(h)[:, 0:HD], start=True, stop=True)
                            return ii
                        if lastsq:
                            EM.O("te", grp_sq, r=[qk, pk], w=[f"bk{b_q}"])
                        else:
                            EM.O("te", grp_sq, r=[qk, pk], w=[f"bk{b_p}", f"bk{b_q}"])
                    if even:
                        pdst = QNG.ap()[:, gg * 32 * L:(gg + 1) * 32 * L].rearrange(
                            "p (b x) -> p b x", x=4 * L)[:, :, 2 * L:3 * L]
                        qdst = Qpr[:, gg * 1024: gg * 1024 + 1024]
                        psrcv = bankspan(0, 1024).rearrange("p (b x) -> p b x", x=L)
                        EM.O("ve", lambda pdst=pdst, psrcv=psrcv: ve.tensor_copy(pdst, psrcv),
                             r=["bk0", "bk1"], w=[pok], presplit=SPL and bool(KSPLM & 4))
                        EM.O("se", lambda qdst=qdst: se.copy(qdst, bankspan(2, 1024)),
                             r=["bk2", "bk3"], w=[qok])
                    else:
                        qdst = QNG.ap()[:, gg * 32 * L:(gg + 1) * 32 * L].rearrange(
                            "p (b x) -> p b x", x=4 * L)[:, :, 0:(HD if lastsq else L)]
                        qsrcv = bankspan(2, 1024).rearrange(
                            "p (b x) -> p b x", x=L)[:, :, 0:(HD if lastsq else L)]
                        if not lastsq:
                            pdst = Ppr[:, gg * 1024: gg * 1024 + 1024]
                            EM.O("ve", lambda pdst=pdst: ve.tensor_copy(
                                pdst, bankspan(0, 1024)),
                                 r=["bk0", "bk1"], w=[pok],
                                 presplit=SPL and bool(KSPLM & 4))
                        EM.O("se", lambda qdst=qdst, qsrcv=qsrcv: se.copy(qdst, qsrcv),
                             r=["bk2", "bk3"], w=[qok])
        for h in range(H):
            b_i = 4 + h % 2
            prt = slice((h % 2) * HD, (h % 2) * HD + HD)
            i_t = h // 2
            Rsl = AR[prt, i_t * 2 * L + L: (i_t + 1) * 2 * L]
            def grp_y(h=h, b_i=b_i, Rsl=Rsl, prt=prt):
                te.matmul(bank(b_i, (h // 2) * HD, HD), Rsl, ssb[prt, ts(h, HD)],
                          start=True, stop=False)
                te.matmul(bank(b_i, (h // 2) * HD, HD),
                          QNG[:, qb(h) + L: qb(h) + 2 * L], Xs[:, xsl(h)],
                          start=False, stop=False)
                return te.matmul(bank(b_i, (h // 2) * HD, HD),
                                 QNG[:, qb(h) + 3 * L: qb(h) + 4 * L],
                                 vt[:, ts(h, HD)], start=False, stop=True)
            EM.O("te", grp_y, r=[ark, "ssb", "QNG", "Xs", vk], w=[f"bk{b_i}"])
        yv4 = y_tm.ap().rearrange("t (h two d) -> t h two d", two=2, d=HD)
        EM.O("ve", lambda yv4=yv4: ve.tensor_copy(yv4[:, :, 0, :], bank(4)), r=["bk4"], w=["y"])
        EM.O("ve", lambda yv4=yv4: ve.tensor_copy(yv4[:, :, 1, :], bank(5)), r=["bk5", "y"], w=["y"])
        EM.mark("yred")
        for half8 in range(2):
            b_i = half8
            def grp_ss(half8=half8, b_i=b_i):
                for k8 in range(8):
                    h = half8 * 8 + k8
                    te.matmul(bankp(b_i, 0, HD, k8 * HD, HD), Btm1[:, ts(h, HD)],
                              Xs[:, xsl(h)], start=True, stop=False)
                    ii = te.matmul(bankp(b_i, 0, HD, k8 * HD, HD), Ktm1[:, ts(h, HD)],
                                   vt[:, ts(h, HD)], start=False, stop=True)
                return ii
            EM.O("te", grp_ss, r=["Btm", "Ktm", "Xs", vk], w=[f"bk{b_i}"])
        EM.O("ve", lambda: ve.tensor_tensor(ss[:, 0:512], bankp(0, 0, HD, 0, 512),
                                            ss[:, 0:512], Op.add),
             r=["bk0"], w=["ss"], presplit=SPL and bool(KSPLM & 8))
        EM.O("ve", lambda: ve.tensor_tensor(ss[:, 512:1024], bankp(1, 0, HD, 0, 512),
                                            ss[:, 512:1024], Op.add),
             r=["bk1"], w=["ss"])
        for h in range(H):
            lcol = lamL[pr2][(h % 2) * HD:(h % 2) * HD + HD, h // 2: h // 2 + 1]
            EM.O("ve", lambda h=h, lcol=lcol: ve.tensor_scalar(ss[:, ts(h, HD)],
                                                               ss[:, ts(h, HD)], lcol, None,
                                                               Op.mult),
                 r=[lk], w=["ss"])
        EM.O("se", lambda: se.copy(ssb[0:HD, :], ss[:, :]), r=["ss"], w=["ssb"])
        EM.O("se", lambda: se.copy(ssb[HD:128, :], ss[:, :]), r=["ss", "ssb"], w=["ssb"])
        EM.mark("bkt")

    def emit_post(c, EM):
        pr2, pr3 = c % 2, c % 3
        xkey = f"x{pr3}"
        vk = f"vtm{pr2}"
        gk = f"gtm{pr2}"
        ck = f"coef{pr2}"
        EM.O("se", lambda: se.activation(ysq[:, :], y_tm[:, :], AF.Square),
             r=["y"], w=["ysq"])
        y3 = y_tm.ap().rearrange("t (h d) -> t h d", d=HD)
        ysq3 = ysq.ap().rearrange("t (h d) -> t h d", d=HD)
        g3a = gnst.ap()[:, 0:H].rearrange("t (h one) -> t h one", one=1)
        g3b = gnst.ap()[:, H:2 * H].rearrange("t (h one) -> t h one", one=1)
        EM.O("ve", lambda y3=y3, g3a=g3a: ve.tensor_reduce(g3a, y3, AX.X, Op.add),
             r=["y"], w=["gnst"])
        EM.O("ve", lambda ysq3=ysq3, g3b=g3b: ve.tensor_reduce(g3b, ysq3, AX.X, Op.add),
             r=["ysq"], w=["gnst"])
        cf_bc = coef[pr2].ap().rearrange("t (h one) -> t h one", one=1).broadcast_to((L, H, HD))
        v3 = vtm[pr2].ap().rearrange("t (h d) -> t h d", d=HD)
        EM.O("ve", lambda ysq3=ysq3, v3=v3, cf_bc=cf_bc: ve.tensor_tensor(
            ysq3, v3, cf_bc, Op.mult), r=[vk, ck, "ysq"], w=["ysq"])
        EM.O("ve", lambda: ve.tensor_scalar(gnst[:, 0:H], gnst[:, 0:H], 1.0 / HD, None,
                                            Op.mult), r=["gnst"], w=["gnst"])
        EM.O("ve", lambda: ve.tensor_scalar(gnst[:, H:2 * H], gnst[:, H:2 * H], 1.0 / HD, None,
                                            Op.mult), r=["gnst"], w=["gnst"])
        EM.O("ve", lambda: ve.tensor_tensor(gnst[:, 2 * H:3 * H], gnst[:, 0:H], gnst[:, 0:H],
                                            Op.mult), r=["gnst"], w=["gnst"])
        EM.O("ve", lambda: ve.tensor_tensor(gnst[:, H:2 * H], gnst[:, H:2 * H],
                                            gnst[:, 2 * H:3 * H], Op.subtract),
             r=["gnst"], w=["gnst"])
        EM.O("se", lambda: se.activation(gnst[:, H:2 * H], gnst[:, H:2 * H], AF.Sqrt,
                                         bias=EPS_GN), r=["gnst", "constt"], w=["gnst"])
        EM.O("ve", lambda: ve.reciprocal(gnst[:, H:2 * H], gnst[:, H:2 * H]),
             r=["gnst"], w=["gnst"])
        EM.O("ve", lambda: ve.scalar_tensor_tensor(gnst[:, 0:H], gnst[:, 0:H], -1.0,
                                                   gnst[:, H:2 * H], Op.mult, Op.mult),
             r=["gnst"], w=["gnst"])
        rs_bc = gnst.ap()[:, H:2 * H].rearrange("t (h one) -> t h one", one=1).broadcast_to((L, H, HD))
        ms_bc = gnst.ap()[:, 0:H].rearrange("t (h one) -> t h one", one=1).broadcast_to((L, H, HD))
        EM.O("ve", lambda y3=y3, rs_bc=rs_bc: ve.tensor_tensor(y3, y3, rs_bc, Op.mult),
             r=["y", "gnst"], w=["y"])
        EM.O("ve", lambda y3=y3, ms_bc=ms_bc: ve.tensor_tensor(y3, y3, ms_bc, Op.add),
             r=["y", "gnst"], w=["y"])
        EM.O("ve", lambda: ve.tensor_tensor(y_tm[:, :], y_tm[:, :], lnw[:, :], Op.mult),
             r=["y", "lnw"], w=["y"])
        EM.O("ve", lambda: ve.tensor_tensor(y_tm[:, :], y_tm[:, :], lnb[:, :], Op.add),
             r=["y", "lnb"], w=["y"])
        EM.O("ve", lambda: ve.tensor_tensor(y_tm[:, :], y_tm[:, :], ysq[:, :], Op.add),
             r=["y", "ysq"], w=["y"])
        EM.O("ve", lambda pr2=pr2: ve.tensor_tensor(yg_tm[:, :], y_tm[:, :], gtm[pr2][:, :],
                                                    Op.mult),
             r=["y", gk], w=["yg"])
        pe_transpose(EM, yg_tm, "yg", yg_ch, "ygch")
        for half in range(2):
            b_i = 2 + half
            def grp_wo(half=half, b_i=b_i):
                for ct in range(NT):
                    ii = te.matmul(bank(b_i), yg_ch[:, ts(ct, L)],
                                   wsl(wo, ct, half * 512, 512),
                                   start=(ct == 0), stop=(ct == NT - 1))
                return ii
            EM.O("te", grp_wo, r=["ygch", "wo"], w=[f"bk{b_i}"])
            EM.O("se", lambda half=half, b_i=b_i: se.copy(att_tm[:, ts(half, 512)], bank(b_i)),
                 r=[f"bk{b_i}"], w=["att"])
        ln_into(EM, att_tm, att_tm, "att", "att", statsQ, "statsQ",
                scr=yg_tm, sck="yg")
        EM.O("ve", lambda pr3=pr3: ve.tensor_tensor(att_tm[:, :], x_tm[pr3][:, :],
                                                    att_tm[:, :], Op.add),
             r=[xkey, "att"], w=["att"])
        EM.DQO(lambda c=c: sy.dma_start(out=x2_rows[c], in_=att_tm[:, :]),
               r=["att"], w=[f"x2r{c}"])

    def emit_quant(c, EM):
        islot = (KCH - 3) % 3
        identv = x_tm[islot][:, 0:128]
        EM.DQO(lambda identv=identv: sy.dma_start(out=identv, in_=D["ident"]),
               w=[f"x{islot}"])
        kblocks = D["ffn_kernel"].rearrange("(j p) c -> j p c", p=128)
        for j in range(NT):
            EM.DQO(lambda j=j: sy.dma_start(out=lam[:, 0:C], in_=kblocks[j]), w=["lam"])
            EM.O("ve", lambda j=j: ve.tensor_reduce(qm[:, j:j + 1], lam[:, 0:C], AX.X, Op.max,
                                                    apply_absolute_value=True),
                 r=["lam"], w=["qmj"])
        EM.O("ve", lambda: ve.tensor_reduce(qm[:, 8:9], qm[:, 0:NT], AX.X, Op.max),
             r=["qmj"], w=["qm8"])
        EM.O("te", lambda identv=identv: te.transpose(bankp(0, 0, 1, 0, 128), qm[:, 8:9],
                                                      identv),
             r=["qm8", f"x{islot}"], w=["bk0"])
        EM.O("ve", lambda: ve.tensor_reduce(qm[0:1, 9:10], bankp(0, 0, 1, 0, 128), AX.X,
                                            Op.max), r=["bk0"], w=["qm9"])
        EM.O("te", lambda: te.matmul(bankp(1, 0, 128, 0, 1), maskqg2[0:1, L:2 * L],
                                     qm[0:1, 9:10], start=True, stop=True),
             r=["maskqg", "qm9"], w=["bk1"])
        EM.O("ve", lambda: ve.tensor_copy(qm[:, 9:10], bankp(1, 0, 128, 0, 1)),
             r=["bk1"], w=["qm9"])
        EM.O("ve", lambda: ve.reciprocal(qm[:, 10:11], qm[:, 9:10]), r=["qm9"], w=["qm10"])
        EM.O("ve", lambda: ve.tensor_scalar(qm[:, 10:11], qm[:, 10:11], 7.0, None, Op.mult),
             r=["qm10"], w=["qm10"])
        EM.O("ve", lambda: ve.tensor_scalar(qm[:, 11:12], qm[:, 9:10], 1.0 / 7.0, None,
                                            Op.mult), r=["qm9"], w=["qm11"])
        for j in range(NT):
            EM.DQO(lambda j=j: sy.dma_start(out=lam[:, 0:C], in_=kblocks[j]), w=["lam"])
            EM.O("ve", lambda: ve.tensor_scalar(lam[:, 0:C], lam[:, 0:C], qm[:, 10:11], None,
                                                Op.mult), r=["lam", "qm10"], w=["lam"])
            EM.O("ve", lambda: ve.tensor_scalar(lam[:, 0:C], lam[:, 0:C], RND_MAGIC, RND_MAGIC,
                                                Op.add, Op.subtract), r=["lam"], w=["lam"])
            EM.O("ve", lambda: ve.tensor_scalar(lam[:, 0:C], lam[:, 0:C], 7.0, -8.0, Op.min,
                                                Op.max), r=["lam"], w=["lam"])
            EM.O("ve", lambda j=j: ve.tensor_scalar(ffnw[:, ts(j, C)], lam[:, 0:C],
                                                    qm[:, 11:12], None, Op.mult),
                 r=["lam", "qm11", "wr"], w=["wr"])

    def emit_ffn_front(c, EM):
        fp = c % 4
        xsrc = fxs[fp]
        xkey = fxk[fp]
        EM.DQO(lambda c=c, xsrc=xsrc: sy.dma_start(out=xsrc[:, :], in_=x2_rows[c]),
               r=[f"x2r{c}"], w=[xkey])
        ln_into(EM, f_xh, xsrc, xkey, "xh_tm", statsP, "statsP")
        pe_transpose(EM, f_xh, "xh_tm", f_xhch[fp], f_xk[fp])

    def emit_ffn_backM(c, EM):
        fp = c % 4
        fo = c % 2
        fok = f_ok[fo]
        def grp_lor(fp=fp):
            for ct in range(NT):
                ii = te.matmul(bankp(6, 0, 16, 0, L), ffna[:, ts(ct, 16)],
                               f_xhch[fp][:, ts(ct, L)], start=(ct == 0), stop=(ct == NT - 1))
            return ii
        EM.O("te", grp_lor, r=[f_xk[fp], "ffna"], w=["bk6"])
        EM.O("se", lambda: se.copy(f_lor[0:16, :], bankp(6, 0, 16, 0, L)),
             r=["bk6"], w=["f_lor2"])
        for half in range(2):
            b_i = 4 + half
            def grp_ffn(half=half, b_i=b_i, fp=fp):
                for ct in range(NT):
                    te.matmul(bank(b_i), f_xhch[fp][:, ts(ct, L)],
                              wsl(ffnw, ct, half * 512, 512),
                              start=(ct == 0), stop=False)
                return te.matmul(bank(b_i), f_lor[:, :], ffnb[:, ts(half, 512)],
                                 start=False, stop=True)
            EM.O("te", grp_ffn, r=[f_xk[fp], "wr", "f_lor", "f_lor2", "ffnb"], w=[f"bk{b_i}"])
            EM.O("se", lambda half=half, b_i=b_i, fo=fo: se.copy(
                f_outb[fo][:, ts(half, 512)], bank(b_i)),
                 r=[f"bk{b_i}"], w=[fok])

    def emit_ffn_backT(c, EM):
        fp = c % 4
        fo = c % 2
        of = c % 2
        xsrc = fxs[fp]
        xkey = fxk[fp]
        ok = o_tk[of]
        fok = f_ok[fo]
        ln_into(EM, f_outb[fo], f_outb[fo], fok, fok, statsQ, "statsQ")
        EM.O("ve", lambda xsrc=xsrc, of=of, fo=fo: ve.tensor_tensor(
            o_tmb[of][:, :], xsrc[:, :], f_outb[fo][:, :], Op.add),
             r=[xkey, fok], w=[ok])
        EM.DQO(lambda c=c, of=of: sy.dma_start(out=out_rows[c], in_=o_tmb[of][:, :]),
               r=[ok], w=[f"outr{of}"])

    with nc.Block() as block:
        for dst, src, b in [(mixv, D["mixv"], "mixv"), (kkv, D["kkv"], "kkv"),
                            (kav, D["kav"], "kav"), (maskqg2, D["maskqg2"], "maskqg"),
                            (xctxc, D["xctx_col"], "xctxc"),
                            (ss, D["s0_in"], "ss"),
                            (selbd, D["selbd"], "selbd")]:
            DQO(lambda dst=dst, src=src: sy.dma_start(out=dst[:, :], in_=src), w=[b])
        DQO(lambda: sy.dma_start(out=pt_sb[:, :], in_=d_par), w=["pt"])
        PT_SEM[0] = wl["pt"]
        blk = lambda ap: ap.rearrange("(j p) c -> p j c", p=128)
        def blkdst(t):
            return t.ap().rearrange("p (j c) -> p j c", j=NT)
        for dst, src, b in [(blkdst(wr), blk(D["W_r"]), "wr"), (blkdst(wk), blk(D["W_k"]), "wk"),
                            (blkdst(wv), blk(D["W_v"]), "wv"), (blkdst(wo), blk(D["W_o"]), "wo"),
                            (blkdst(w1), blk(D["w1"]), "w1"), (blkdst(a1), blk(D["a1"]), "a1"),
                            (blkdst(g1), blk(D["g1"]), "g1"), (blkdst(ffna), blk(D["ffn_a"]), "ffna"),
                            (w2a[:, :], D["w2aug"], "w2a"), (a2a[:, :], D["a2aug"], "a2a"),
                            (g2[:, :], D["g2"], "g2"), (ffnb[:, :], D["ffn_baug"], "ffnb"),
                            (rkbd[:, :], D["rkbd"], "rkbd"), (lnw[:, :], D["lnw_rep"], "lnw"),
                            (lnb[:, :], D["lnb_rep"], "lnb"), (bo2[:, :], D["bo2"], "bo2"),
                            (eyebf[:, :], D["ident"], "eye")]:
            O("gq", lambda dst=dst, src=src: gp_raw.dma_start(out=dst, in_=src), w=[b])
        O("ve", lambda: ve.memset(constt[:, 0:1], EPS_LN), w=["constt"])
        O("ve", lambda: ve.memset(constt[:, 1:2], EPS_GN), r=["constt"], w=["constt"])
        O("ve", lambda: ve.memset(constt[:, 2:3], NEG_EXP_HALF), r=["constt"], w=["constt"])
        O("ve", lambda: ve.memset(constt[:, 3:4], 1.0), r=["constt"], w=["constt"])
        O("ve", lambda: ve.memset(constt[:, 4:5], POS_EXP_HALF), r=["constt"], w=["constt"])
        O("ve", lambda: ve.memset(tanht[HD:HD + 1, :], 1.0), w=["tanht_one"])
        O("ve", lambda: ve.memset(a1t[HD:HD + 1, :], 1.0), w=["a1t_one"])
        O("ve", lambda: ve.memset(f_lor[:, :], 1.0), w=["f_lor"])
        O("ve", lambda: ve.tensor_scalar(kav1m[:, :], kav[:, :], -1.0, 1.0, Op.mult, Op.add),
          r=["kav"], w=["kav1m"])

        O("se", lambda: se.activation(lamL[0][:, :], xctxc[:, :], AF.Square),
          r=["xctxc"], w=["lamL0"])
        O("te", lambda: te.matmul(bankp(0, 0, 1, 0, NT), constt[:, 3:4], xctxc[:, :],
                                  start=True, stop=True), r=["constt", "xctxc"], w=["bk0"])
        O("te", lambda: te.matmul(bankp(0, 0, 1, NT, NT), constt[:, 3:4], lamL[0][:, :],
                                  start=True, stop=True), r=["constt", "lamL0"], w=["bk0"])
        O("ve", lambda: ve.tensor_reduce(qm[0:1, 6:7], bankp(0, 0, 1, 0, NT), AX.X, Op.add),
          r=["bk0"], w=["qm6"])
        O("ve", lambda: ve.tensor_reduce(qm[0:1, 5:6], bankp(0, 0, 1, NT, NT), AX.X, Op.add),
          r=["bk0"], w=["qm5"])
        O("te", lambda: te.matmul(bankp(1, 0, 128, 0, 2), maskqg2[0:1, L:2 * L], qm[0:1, 5:7],
                                  start=True, stop=True), r=["maskqg", "qm5", "qm6"], w=["bk1"])
        O("ve", lambda: ve.tensor_copy(qm[:, 5:7], bankp(1, 0, 128, 0, 2)),
          r=["bk1"], w=["qm5", "qm6"])
        O("ve", lambda: ve.tensor_scalar(qm[:, 6:7], qm[:, 6:7], 1.0 / C, None, Op.mult),
          r=["qm6"], w=["qm6"])
        O("ve", lambda: ve.tensor_scalar(qm[:, 5:6], qm[:, 5:6], 1.0 / C, None, Op.mult),
          r=["qm5"], w=["qm5"])
        O("ve", lambda: ve.scalar_tensor_tensor(qm[:, 7:8], qm[:, 6:7], qm[:, 6:7], qm[:, 5:6],
                                                Op.mult, Op.subtract),
          r=["qm5", "qm6"], w=["qm7"])
        O("ve", lambda: ve.tensor_scalar(qm[:, 7:8], qm[:, 7:8], -1.0, EPS_LN, Op.mult, Op.add),
          r=["qm7"], w=["qm7"])
        O("se", lambda: se.activation(qm[:, 7:8], qm[:, 7:8], AF.Sqrt), r=["qm7"], w=["qm7"])
        O("ve", lambda: ve.reciprocal(qm[:, 7:8], qm[:, 7:8]), r=["qm7"], w=["qm7"])
        O("ve", lambda: ve.scalar_tensor_tensor(qm[:, 4:5], qm[:, 6:7], -1.0, qm[:, 7:8],
                                                Op.mult, Op.mult), r=["qm6", "qm7"], w=["qm4"])
        O("ve", lambda: ve.tensor_scalar(xctx_hat[:, :], xctxc[:, :], qm[:, 7:8], qm[:, 4:5],
                                         Op.mult, Op.add), r=["xctxc", "qm7", "qm4"], w=["xctxh"])

        emit_prep(0, DIR)
        for c in range(KCH):
            streams = [record(emit_scan, c)]
            if c + 1 < KCH:
                streams.append(record(emit_prep, c + 1))
            if c >= 1:
                streams.append(record(emit_post, c - 1))
            if c == KCH - 1:
                streams.append(record(emit_quant, 0))
            merge(streams)
        emit_post(KCH - 1, DIR)

        with gp_raw.register("pr_g") as pr_g:
            gp_raw.wait_ge(SH["gq"], CN["gq"])
            gp_raw.wait_ge(SH[PT_SEM[0][0]], PT_SEM[0][1])
            gp_raw.dma_start(out=D["cc_in"], in_=D["s0_in"]).then_inc(SH["czq"], 16)
            gp_raw.wait_ge(SH["czq"], 16)
            gp_raw.reg_load(pr_g, pt_sb[0:1, 0:1])
            with gp_raw.If_eq(pr_g, 0):
                gp_raw.wait_ge(SH["se"], CN["se"])
                gp_raw.dma_start(out=D["cc_in"], in_=ss[:, :]).then_inc(SH["s0q"], 16)
                gp_raw.wait_ge(SH["s0q"], 16)
        gp_raw.collective_compute("AllGather", Op.bypass,
                                  replica_groups=[[0, 1], [2, 3], [4, 5], [6, 7]],
                                  ins=[D["cc_in"]], outs=[D["cc_out"]]).then_inc(SH["cc"], 1)
        CN["cc"] += 1

        DQO(lambda: sy.dma_start(out=d_send, in_=ss[:, :]), r=["ss"], w=["send"])


        emit_ffn_front(0, DIR)
        if KCH > 1:
            emit_ffn_front(1, DIR)
        for c in range(KCH):
            streams = [record(emit_ffn_backM, c)]
            if c >= 1:
                streams.append(record(emit_ffn_backT, c - 1))
            if c + 2 < KCH:
                streams.append(record(emit_ffn_front, c + 2))
            merge(streams)
        emit_ffn_backT(KCH - 1, DIR)

        for i in range(NDQ):
            s = f"dq{i}"
            if CN[s] > 0:
                sy.wait_ge(SH[s], CN[s])
        gp_raw.wait_ge(SH["gq"], CN["gq"])

    lp.__exit__(None, None, None)
    es.close()
    return nc


def make_core_inputs(inputs, SEG):
    g = lambda n: np.ascontiguousarray(inputs[n], dtype=np.float32)
    x = g("L0_attn_x")
    base = {}
    base["W_r"] = g("L0_attn_W_r"); base["W_k"] = g("L0_attn_W_k")
    base["W_v"] = g("L0_attn_W_v"); base["W_o"] = g("L0_attn_W_o")
    base["w1"] = g("L0_attn_w1")
    base["w2aug"] = np.ascontiguousarray(
        np.concatenate([g("L0_attn_w2"), g("L0_attn_w0")[None, :]], axis=0))
    base["a1"] = g("L0_attn_a1")
    base["a2aug"] = np.ascontiguousarray(
        np.concatenate([g("L0_attn_a2"), g("L0_attn_a0")[None, :]], axis=0))
    base["g1"] = g("L0_attn_g1"); base["g2"] = g("L0_attn_g2")
    mixv = np.zeros((128, 6 * NT), np.float32)
    for mi, nm in enumerate(["x_r", "x_w", "x_k", "x_v", "x_a", "x_g"]):
        mixv[:, mi * NT:(mi + 1) * NT] = g("L0_attn_" + nm).reshape(NT, 128).T
    base["mixv"] = mixv
    base["kkv"] = np.ascontiguousarray(g("L0_attn_k_k").reshape(NT, 128).T)
    base["kav"] = np.ascontiguousarray(g("L0_attn_k_a").reshape(NT, 128).T)
    rk = g("L0_attn_r_k")
    rkbd = np.zeros((128, 2 * NT), np.float32)
    for i in range(NT):
        rkbd[0:64, 2 * i] = rk[2 * i]
        rkbd[64:128, 2 * i + 1] = rk[2 * i + 1]
    base["rkbd"] = rkbd
    base["lnw_rep"] = np.broadcast_to(g("L0_attn_ln_w")[None, :], (128, C)).copy()
    base["lnb_rep"] = np.broadcast_to(g("L0_attn_ln_b")[None, :], (128, C)).copy()
    bo2 = np.zeros((128, 2), np.float32); bo2[0:64, 0] = 1; bo2[64:128, 1] = 1
    base["bo2"] = bo2
    selbd = np.zeros((2, 128), np.float32); selbd[0, 0:64] = 1; selbd[1, 64:128] = 1
    base["selbd"] = selbd
    base["ident"] = np.eye(128, dtype=np.float32)
    mq = np.triu(np.ones((L, L), np.float32), 1)
    mg = np.triu(np.ones((L, L), np.float32), 0)
    mp = np.tril(np.ones((L, L), np.float32), -1)
    base["maskqg2"] = np.ascontiguousarray(np.concatenate([mq, mg, mq, mg, mp], axis=1))
    base["ffn_kernel"] = g("L0_ffn_kernel")
    base["ffn_a"] = g("L0_ffn_a")
    base["ffn_baug"] = np.ascontiguousarray(
        np.concatenate([g("L0_ffn_b"), g("L0_ffn_bias")[None, :]], axis=0))

    maps = []
    B = x.shape[0]
    for core in range(2 * B):
        b, half = core // 2, core % 2
        m = dict(base)
        m["x_seg"] = np.ascontiguousarray(x[b, half * SEG:(half + 1) * SEG])
        if half == 0:
            m["xctx_col"] = np.zeros((128, NT), np.float32)
        else:
            m["xctx_col"] = np.ascontiguousarray(x[b, half * SEG - 1].reshape(NT, 128).T)
        m["s0_in"] = np.zeros((HD, H * HD), np.float32)
        m["par"] = np.array([[half]], np.int32)
        maps.append(m)
    return maps


def kernel(**inputs):
    from concourse.bass_utils import run_bass_kernel_spmd
    SEG = inputs["L0_attn_x"].shape[1] // 2
    maps = make_core_inputs(inputs, SEG)
    nc = build_nc(SEG)
    res = run_bass_kernel_spmd(nc, maps, core_ids=list(range(len(maps))))
    B, T, _ = inputs["L0_attn_x"].shape
    out = np.zeros((B, T, C), np.float32)
    for core in range(len(maps)):
        b, half = core // 2, core % 2
        out[b, half * SEG:(half + 1) * SEG] = res.results[core]["out_seg"]
    return out
